# version 03
# baseline (speedup 1.0000x reference)
import numpy as np
import ml_dtypes
import concourse.bass as bass
import concourse.mybir as mybir
from concourse.bass_utils import run_bass_kernel_spmd
from contextlib import ExitStack

F32 = mybir.dt.float32
BF16 = mybir.dt.bfloat16
AF = mybir.ActivationFunctionType
ALU = mybir.AluOpType
BF = ml_dtypes.bfloat16

D = 1024
S = 4096
NCORE = 8
DC = D // 128
SW = 512
SC = S // SW
HID = 2048
HC = HID // 128
LNEPS = 1e-5

AWORDS = 50968
BIG0 = 0
BIG1 = 16384
RS = 32768


class Sem:
    __slots__ = ("h", "n")

    def __init__(self, nc, name):
        self.h = nc.alloc_semaphore(name)
        self.n = 0


class Prog:
    def __init__(self, nc, ctx):
        self.nc = nc
        self.ctx = ctx
        e = ctx.enter_context
        self.s_mm = Sem(nc, "s_mm")
        self.s_ev = Sem(nc, "s_ev")
        self.s_sc = Sem(nc, "s_sc")
        self.s_ve = Sem(nc, "s_ve")
        self.s_po = Sem(nc, "s_po")
        self.s_ld = [Sem(nc, "s_ld0"), Sem(nc, "s_ld1")]
        self.s_gd = [Sem(nc, "s_gd0"), Sem(nc, "s_gd1")]
        self.s_ad = [Sem(nc, "s_ad0"), Sem(nc, "s_ad1")]
        self.pp = [e(nc.psum_tensor(f"pp{i}", [128, SW], F32)) for i in range(4)]
        self.ps_kv = e(nc.psum_tensor("ps_kv", [128, 704], F32))
        self.ps_tr = e(nc.psum_tensor("ps_tr", [128, 128], BF16))
        self.arena = e(nc.sbuf_tensor("arena", [128, AWORDS], F32))
        self.ones128 = e(nc.sbuf_tensor("ones128", [128, 128], BF16))
        self.zeros_w = e(nc.sbuf_tensor("zeros_w", [128, S], BF16))
        self.colb = e(nc.sbuf_tensor("colb", [128, 16], F32))
        self.small = e(nc.sbuf_tensor("small", [128, 64], F32))
        self.cst = e(nc.sbuf_tensor("cst", [128, 8], F32))
        self.zc = e(nc.sbuf_tensor("zc", [128, 32], F32))
        nc.const_aps.register(F32, LNEPS, self.cst[:, 0:1])
        nc.const_aps.register(F32, 1.0, self.cst[:, 1:2])
        nc.const_aps.register(F32, 1e-30, self.cst[:, 2:3])
        for s_ in (self.s_mm, self.s_ev, self.s_sc, self.s_ve, self.s_po,
                   *self.s_ld, *self.s_gd, *self.s_ad):
            nc.gpsimd.sem_clear(s_.h)
        nc.all_engine_barrier()

    def f32(self, off, n=4096):
        assert off + n <= AWORDS
        return self.arena[:, off:off + n]

    def f32_3(self, off, k, s):
        assert off + k * s <= AWORDS
        return self.arena[:, off:off + k * s].rearrange("p (k s) -> p k s", s=s)

    def bf(self, off, nbf):
        assert off + nbf // 2 <= AWORDS
        return self.arena[:, off:off + nbf // 2].bitcast(BF16)

    def bf3(self, off, k, s):
        assert off + k * s // 2 <= AWORDS
        return self.arena[:, off:off + k * s // 2].bitcast(BF16) \
            .rearrange("p (k s) -> p k s", s=s)

    def barrier(self):
        self.nc.all_engine_barrier()


def run_block(nc, order, bodies):
    with nc.Block() as blk:
        for name in order:
            if name in bodies:
                getattr(blk, name)(bodies[name])



def mirror_tensor(nc, t):
    return nc.dram_tensor(t.name + "_bf", list(t.shape), BF16, kind="Internal")


def emit_mirror_prelude(p, pairs3, stage_off):
    nc = p.nc
    stage = p.f32_3(stage_off, 2, 2048)
    stb = [p.bf(stage_off + 4096 + 1024 * i, 2048) for i in range(2)]
    ld0 = [p.s_ld[0].n, p.s_ld[1].n]
    gd0 = [p.s_gd[0].n, p.s_gd[1].n]
    ve0 = p.s_ve.n
    n = len(pairs3)

    def sync_body(sync):
        for i, (srcap, dstap) in enumerate(pairs3):
            w = srcap.shape[-1]
            pr = srcap.shape[0]
            if i >= 2:
                sync.wait_ge(p.s_ve.h, ve0 + i - 1)
            sync.dma_start(stage[0:pr, i % 2, 0:w], srcap) \
                .then_inc(p.s_ld[i % 2].h, 16)

    def vec_body(vector):
        for i, (srcap, dstap) in enumerate(pairs3):
            w = srcap.shape[-1]
            pr = srcap.shape[0]
            vector.wait_ge(p.s_ld[i % 2].h, ld0[i % 2] + 16 * (i // 2 + 1))
            if i >= 2:
                vector.wait_ge(p.s_gd[i % 2].h, gd0[i % 2] + 16 * (i // 2))
            vector.tensor_copy(stb[i % 2][0:pr, 0:w], stage[0:pr, i % 2, 0:w]) \
                .then_inc(p.s_ve.h, 1)

    def gp_body(gpsimd):
        for i, (srcap, dstap) in enumerate(pairs3):
            w = srcap.shape[-1]
            pr = srcap.shape[0]
            gpsimd.wait_ge(p.s_ve.h, ve0 + i + 1)
            gpsimd.dma_start(dstap, stb[i % 2][0:pr, 0:w]) \
                .then_inc(p.s_gd[i % 2].h, 16)
        gpsimd.wait_ge(p.s_gd[0].h, gd0[0] + 16 * ((n + 1) // 2))
        gpsimd.wait_ge(p.s_gd[1].h, gd0[1] + 16 * (n // 2))

    run_block(nc, ("sync", "vector", "gpsimd"),
              {"sync": sync_body, "vector": vec_body, "gpsimd": gp_body})
    p.s_ld[0].n = ld0[0] + 16 * ((n + 1) // 2)
    p.s_ld[1].n = ld0[1] + 16 * (n // 2)
    p.s_gd[0].n = gd0[0] + 16 * ((n + 1) // 2)
    p.s_gd[1].n = gd0[1] + 16 * (n // 2)
    p.s_ve.n = ve0 + n
    p.barrier()


def emit_load_bf(p, pairs):
    nc = p.nc
    sd = p.s_ld[0]
    ld0 = sd.n

    def sync_body(sync):
        for s_, d_ in pairs:
            sync.dma_start(d_, s_).then_inc(sd.h, 16)
        sync.wait_ge(sd.h, ld0 + 16 * len(pairs))

    run_block(nc, ("sync",), {"sync": sync_body})
    sd.n = ld0 + 16 * len(pairs)
    p.barrier()


def emit_load_cast(p, pairs, stage_off):
    direct = [(s_, d_) for (s_, d_) in pairs if d_.dtype == F32]
    casts = [(s_, d_) for (s_, d_) in pairs if d_.dtype != F32]
    nc = p.nc
    if direct:
        sd = p.s_ld[0]
        ld0 = sd.n

        def sync_d(sync):
            for s_, d_ in direct:
                sync.dma_start(d_, s_).then_inc(sd.h, 16)
            sync.wait_ge(sd.h, ld0 + 16 * len(direct))

        run_block(nc, ("sync",), {"sync": sync_d})
        sd.n = ld0 + 16 * len(direct)
    if casts:
        stage = p.f32_3(stage_off, 2, 2048)
        ld0 = [p.s_ld[0].n, p.s_ld[1].n]
        ve0 = p.s_ve.n

        def sync_body(sync):
            for i, (src, dst) in enumerate(casts):
                w = src.shape[-1]
                pr = src.shape[0] if len(src.shape) > 1 else 128
                if i >= 2:
                    sync.wait_ge(p.s_ve.h, ve0 + i - 1)
                sync.dma_start(stage[0:pr, i % 2, 0:w], src) \
                    .then_inc(p.s_ld[i % 2].h, 16)

        def vec_body(vector):
            for i, (src, dst) in enumerate(casts):
                w = src.shape[-1]
                pr = src.shape[0] if len(src.shape) > 1 else 128
                vector.wait_ge(p.s_ld[i % 2].h, ld0[i % 2] + 16 * (i // 2 + 1))
                vector.tensor_copy(dst, stage[0:pr, i % 2, 0:w]).then_inc(p.s_ve.h, 1)

        run_block(nc, ("sync", "vector"), {"sync": sync_body, "vector": vec_body})
        n_c = len(casts)
        p.s_ld[0].n = ld0[0] + 16 * ((n_c + 1) // 2)
        p.s_ld[1].n = ld0[1] + 16 * (n_c // 2)
        p.s_ve.n = ve0 + n_c
    p.barrier()


def emit_proj(p, *, n_o, n_k, lhsT, rhs, evac_cb, evac_engine="scalar", n_s=SC):
    nc = p.nc
    mm0, ev0 = p.s_mm.n, p.s_ev.n

    def tensor_body(tensor):
        for o in range(n_o):
            for s in range(n_s):
                it = o * n_s + s
                ps = p.pp[it % 2]
                if it >= 2:
                    tensor.wait_ge(p.s_ev.h, ev0 + it - 1)
                for k in range(n_k):
                    mm = tensor.matmul(ps[:, :], lhsT(o, k), rhs(k, s),
                                       start=(k == 0), stop=(k == n_k - 1))
                mm.then_inc(p.s_mm.h, 1)

    def evac_body(eng):
        for o in range(n_o):
            for s in range(n_s):
                it = o * n_s + s
                eng.wait_ge(p.s_mm.h, mm0 + it + 1)
                evac_cb(eng, o, s, p.pp[it % 2]).then_inc(p.s_ev.h, 1)

    run_block(nc, ("tensor", evac_engine),
              {"tensor": tensor_body, evac_engine: evac_body})
    p.s_mm.n = mm0 + n_o * n_s
    p.s_ev.n = ev0 + n_o * n_s
    p.barrier()


def emit_stat_pass(p, src3, dst_row, n_d=DC, inv_n=1.0 / D):
    nc = p.nc
    mm0, ev0 = p.s_mm.n, p.s_ev.n

    def tensor_body(tensor):
        for s in range(SC):
            ps = p.pp[s % 2]
            if s >= 2:
                tensor.wait_ge(p.s_ev.h, ev0 + s - 1)
            for d in range(n_d):
                mm = tensor.matmul(ps[:, :], p.ones128[:, :],
                                   src3[:, d, bass.ts(s, SW)],
                                   start=(d == 0), stop=(d == n_d - 1))
            mm.then_inc(p.s_mm.h, 1)

    def evac_body(scalar):
        for s in range(SC):
            scalar.wait_ge(p.s_mm.h, mm0 + s + 1)
            scalar.activation(dst_row[:, bass.ts(s, SW)], p.pp[s % 2][:, :],
                              AF.Identity, scale=inv_n).then_inc(p.s_ev.h, 1)

    run_block(nc, ("tensor", "scalar"), {"tensor": tensor_body, "scalar": evac_body})
    p.s_mm.n = mm0 + SC
    p.s_ev.n = ev0 + SC
    p.barrier()


def emit_sq_inplace(p, buf3, n_d=DC):
    sc0 = p.s_sc.n

    def sq(scalar):
        for d in range(n_d):
            scalar.activation(buf3[:, d, :], buf3[:, d, :], AF.Square)
        scalar.nop().then_inc(p.s_sc.h, 1)

    run_block(p.nc, ("scalar",), {"scalar": sq})
    p.s_sc.n = sc0 + 1
    p.barrier()


def emit_var_rstd(p, m_b, sx2_b, r_b):
    nc = p.nc
    ve0, sc0 = p.s_ve.n, p.s_sc.n

    def vec_var(vector):
        vector.tensor_tensor(r_b[:, :], m_b[:, :], m_b[:, :], ALU.mult)
        vector.tensor_sub(sx2_b[:, :], sx2_b[:, :], r_b[:, :]).then_inc(p.s_ve.h, 1)

    def sc_rstd(scalar):
        scalar.wait_ge(p.s_ve.h, ve0 + 1)
        scalar.activation(r_b[:, :], sx2_b[:, :], AF.Ln, bias=LNEPS)
        scalar.activation(r_b[:, :], r_b[:, :], AF.Exp, scale=-0.5) \
            .then_inc(p.s_sc.h, 1)

    run_block(nc, ("vector", "scalar"), {"vector": vec_var, "scalar": sc_rstd})
    p.s_ve.n = ve0 + 1
    p.s_sc.n = sc0 + 1
    p.barrier()


def emit_stream_stats(p, xT_in, xbf3, m_b, r_b, sx2_b, xs0, xs1):
    nc = p.nc
    ld0 = [p.s_ld[0].n, p.s_ld[1].n]
    ve0 = p.s_ve.n

    def ph1_sync(sync):
        for d in range(DC):
            if d >= 2:
                sync.wait_ge(p.s_ve.h, ve0 + d - 1)
            sync.dma_start((xs0, xs1)[d % 2], xT_in[d]) \
                .then_inc(p.s_ld[d % 2].h, 16)

    def ph1_vec(vector):
        for d in range(DC):
            vector.wait_ge(p.s_ld[d % 2].h, ld0[d % 2] + 16 * (d // 2 + 1))
            vector.tensor_copy(xbf3[:, d, :], (xs0, xs1)[d % 2]).then_inc(p.s_ve.h, 1)

    run_block(nc, ("sync", "vector"), {"sync": ph1_sync, "vector": ph1_vec})
    p.s_ld[0].n = ld0[0] + 16 * (DC // 2)
    p.s_ld[1].n = ld0[1] + 16 * (DC // 2)
    p.s_ve.n = ve0 + DC
    p.barrier()

    emit_stat_pass(p, xbf3, m_b)
    emit_sq_inplace(p, xbf3)
    emit_stat_pass(p, xbf3, sx2_b)
    emit_var_rstd(p, m_b, sx2_b, r_b)


def emit_apply_ln(p, xT_in, xn3, m_b, r_b, t_ap, xs0, xs1):
    nc = p.nc
    ld0 = [p.s_ld[0].n, p.s_ld[1].n]
    ve0 = p.s_ve.n

    def ap_sync(sync):
        for d in range(DC):
            if d >= 2:
                sync.wait_ge(p.s_ve.h, ve0 + 2 * (d - 1))
            sync.dma_start((xs0, xs1)[d % 2], xT_in[d]) \
                .then_inc(p.s_ld[d % 2].h, 16)

    def ap_vec(vector):
        for d in range(DC):
            vector.wait_ge(p.s_ld[d % 2].h, ld0[d % 2] + 16 * (d // 2 + 1))
            vector.tensor_sub(t_ap[:, :], (xs0, xs1)[d % 2], m_b[:, :])
            vector.tensor_mul(xn3[:, d, :], t_ap[:, :], r_b[:, :]) \
                .then_inc(p.s_ve.h, 2)

    run_block(nc, ("sync", "vector"), {"sync": ap_sync, "vector": ap_vec})
    p.s_ld[0].n = ld0[0] + 16 * (DC // 2)
    p.s_ld[1].n = ld0[1] + 16 * (DC // 2)
    p.s_ve.n = ve0 + 2 * DC
    p.barrier()


def emit_dump(p, dram_ap_list, buf3, n=DC):
    sg = p.s_gd[0]
    gd0 = sg.n

    def gp(gpsimd):
        for d in range(n):
            gpsimd.dma_start(dram_ap_list[d], buf3[:, d, :]).then_inc(sg.h, 16)
        gpsimd.wait_ge(sg.h, gd0 + 16 * n)

    run_block(p.nc, ("gpsimd",), {"gpsimd": gp})
    sg.n = gd0 + 16 * n
    p.barrier()


def emit_block0(p, xT_in, xT_out, W, Wb, h_spill, dbg=None):
    nc = p.nc

    sd0 = p.s_ld[0]
    ld0 = sd0.n

    def init_gp(gpsimd):
        gpsimd.memset(p.ones128[:, :], 1.0)
        gpsimd.memset(p.zeros_w[:, :], 0.0)
        gpsimd.memset(p.cst[:, 0:1], LNEPS)
        gpsimd.memset(p.cst[:, 1:2], 1.0)
        gpsimd.memset(p.cst[:, 2:3], 1e-30)

    def init_sync(sync):
        sync.dma_start(p.small[:, 0:24], W["conv_w"]).then_inc(sd0.h, 16)
        sync.dma_start(p.small[:, 24:32], W["conv_b8"]).then_inc(sd0.h, 16)
        sync.wait_ge(sd0.h, ld0 + 32)

    run_block(nc, ("gpsimd", "sync"), {"gpsimd": init_gp, "sync": init_sync})
    sd0.n = ld0 + 32
    p.barrier()

    xn = p.bf3(BIG0, DC, S)
    xs0 = p.f32(BIG1, 4096)
    xs1 = p.f32(BIG1 + 4096, 4096)
    sx2_b = p.f32(BIG1 + 8192, 4096)
    t_ap = p.f32(BIG1 + 12288, 4096)
    m_b = p.f32(RS, 4096)
    r_b = p.f32(RS + 4096, 4096)
    emit_stream_stats(p, xT_in, xn, m_b, r_b, sx2_b, xs0, xs1)
    emit_apply_ln(p, xT_in, xn, m_b, r_b, t_ap, xs0, xs1)
    if dbg:
        emit_dump(p, dbg["xn"], xn)

    w1b = p.bf3(RS + 8192, DC, D)
    emit_load_bf(p, [(Wb["w1T"][k], w1b[:, k, :]) for k in range(DC)])
    x0 = p.bf3(BIG1, DC, S)
    emit_proj(
        p, n_o=DC, n_k=DC,
        lhsT=lambda o, k: w1b[:, k, bass.ts(o, 128)],
        rhs=lambda k, s: xn[:, k, bass.ts(s, SW)],
        evac_cb=lambda eng, o, s, ps: eng.activation(
            x0[:, o, bass.ts(s, SW)], ps[:, :], AF.Identity),
    )
    if dbg:
        emit_dump(p, dbg["x0"], x0)

    x1 = p.bf3(BIG0, DC, S)
    t_a, t_b, t_c = p.f32(RS, 4096), p.f32(RS + 4096, 4096), p.f32(RS + 8192, 4096)
    cw = p.small
    ve0, sc0 = p.s_ve.n, p.s_sc.n

    def conv_vec(vector):
        for d in range(DC):
            if d >= 1:
                vector.wait_ge(p.s_sc.h, sc0 + d)
            x0c = x0[:, d, :]
            vector.tensor_scalar(t_a[:, :], x0c, cw[:, 3 * d + 1:3 * d + 2],
                                 cw[:, 24 + d:25 + d], ALU.mult, ALU.add)
            vector.scalar_tensor_tensor(t_b[:, 1:S], x0c[:, 0:S - 1],
                                        cw[:, 3 * d:3 * d + 1], t_a[:, 1:S],
                                        ALU.mult, ALU.add)
            vector.tensor_copy(t_b[:, 0:1], t_a[:, 0:1])
            vector.scalar_tensor_tensor(t_c[:, 0:S - 1], x0c[:, 1:S],
                                        cw[:, 3 * d + 2:3 * d + 3],
                                        t_b[:, 0:S - 1], ALU.mult, ALU.add)
            vector.tensor_copy(t_c[:, S - 1:S], t_b[:, S - 1:S]) \
                .then_inc(p.s_ve.h, 1)

    def conv_sc(scalar):
        for d in range(DC):
            scalar.wait_ge(p.s_ve.h, ve0 + d + 1)
            scalar.activation(x1[:, d, :], t_c[:, :], AF.Silu).then_inc(p.s_sc.h, 1)

    run_block(nc, ("vector", "scalar"), {"vector": conv_vec, "scalar": conv_sc})
    p.s_ve.n = ve0 + DC
    p.s_sc.n = sc0 + DC
    p.barrier()
    if dbg:
        emit_dump(p, dbg["x1"], x1)

    F1 = p.f32(BIG1, 4096)
    F2 = p.f32(BIG1 + 4096, 4096)
    F3 = p.f32(BIG1 + 8192, 4096)
    bf_a = p.bf(RS, 4096)
    bf_b = p.bf(RS + 2048, 4096)
    bf_c = p.bf(RS + 4096, 4096)
    wt = p.bf3(RS + 6144, 24, 128)
    for c in range(HC):
        pairs = []
        for k in range(DC):
            pairs.append((Wb["dlT"][k][c], wt[:, k, :]))
            pairs.append((Wb["AT"][k][c], wt[:, 8 + k, :]))
            pairs.append((Wb["BT"][k][c], wt[:, 16 + k, :]))
        pairs.append((W["dlb"][c], p.colb[:, 0:1]))
        pairs.append((W["Ab"][c], p.colb[:, 1:2]))
        pairs.append((W["Bb"][c], p.colb[:, 2:3]))
        emit_load_bf(p, pairs)

        emit_proj(
            p, n_o=3, n_k=DC,
            lhsT=lambda o, k: wt[:, 8 * o + k, :],
            rhs=lambda k, s: x1[:, k, bass.ts(s, SW)],
            evac_cb=lambda eng, o, s, ps: (
                eng.activation(F1[:, bass.ts(s, SW)], ps[:, :], AF.Exp,
                               bias=p.colb[:, 0:1]) if o == 0 else
                eng.activation(bf_a[:, bass.ts(s, SW)], ps[:, :], AF.Identity,
                               bias=p.colb[:, 1:2]) if o == 1 else
                eng.activation(bf_b[:, bass.ts(s, SW)], ps[:, :], AF.Identity,
                               bias=p.colb[:, 2:3])),
        )

        sc0, ve0, gd0 = p.s_sc.n, p.s_ve.n, p.s_gd[0].n

        def scan_sc(scalar):
            scalar.activation(F2[:, :], F1[:, :], AF.Ln, bias=1.0) \
                .then_inc(p.s_sc.h, 1)
            scalar.wait_ge(p.s_ve.h, ve0 + 1)
            scalar.activation(F3[:, :], F1[:, :], AF.Exp, scale=-1.0) \
                .then_inc(p.s_sc.h, 1)

        def scan_vec(vector):
            vector.wait_ge(p.s_sc.h, sc0 + 1)
            vector.tensor_tensor(F1[:, :], F2[:, :], bf_a[:, :], ALU.mult) \
                .then_inc(p.s_ve.h, 1)
            vector.wait_ge(p.s_sc.h, sc0 + 2)
            vector.tensor_tensor_scan(F2[:, :], F3[:, :], p.zeros_w[:, :],
                                      1.0, ALU.mult, ALU.add)
            vector.tensor_tensor(F1[:, :], bf_b[:, :], F2[:, :], ALU.mult)
            vector.tensor_tensor_scan(bf_c[:, :], F1[:, :], p.zeros_w[:, :],
                                      0.0, ALU.add, ALU.add).then_inc(p.s_ve.h, 1)

        def scan_gp(gpsimd, c=c):
            gpsimd.wait_ge(p.s_ve.h, ve0 + 2)
            gpsimd.dma_start(h_spill[c], bf_c[:, :]).then_inc(p.s_gd[0].h, 16)
            gpsimd.wait_ge(p.s_gd[0].h, gd0 + 16)

        run_block(nc, ("scalar", "vector", "gpsimd"),
                  {"scalar": scan_sc, "vector": scan_vec, "gpsimd": scan_gp})
        p.s_sc.n = sc0 + 2
        p.s_ve.n = ve0 + 2
        p.s_gd[0].n = gd0 + 16
        p.barrier()

    Cb_ = p.bf3(RS, HC, D)
    Db_ = p.bf3(RS + 8192, DC, D)
    pairs = [(Wb["CT"][k], Cb_[:, k, :]) for k in range(HC)]
    pairs += [(Wb["DT"][k], Db_[:, k, :]) for k in range(DC)]
    pairs.append((W["CDb"], p.colb[:, 0:16]))
    emit_load_bf(p, pairs)
    ve0 = p.s_ve.n

    def cb_vec(vector):
        vector.tensor_add(p.colb[:, 0:8], p.colb[:, 0:8], p.colb[:, 8:16]) \
            .then_inc(p.s_ve.h, 1)

    run_block(nc, ("vector",), {"vector": cb_vec})
    p.s_ve.n = ve0 + 1
    p.barrier()

    hsl = p.bf3(RS + 12288, HC, SW)
    ssm = p.bf3(BIG1, DC, S)
    mm0, ev0 = p.s_mm.n, p.s_ev.n
    gd0 = [p.s_gd[0].n, p.s_gd[1].n]

    def ph5_gp(gpsimd):
        for s in range(SC):
            if s >= 1:
                gpsimd.wait_ge(p.s_mm.h, mm0 + s * DC)
            for c2 in range(HC):
                gpsimd.dma_start(
                    hsl[:, c2, :],
                    h_spill[c2].rearrange("p (sc w) -> p sc w", w=SW)[:, s, :]
                ).then_inc(p.s_gd[s % 2].h, 16)

    def ph5_tensor(tensor):
        it = 0
        for s in range(SC):
            tensor.wait_ge(p.s_gd[s % 2].h,
                           gd0[s % 2] + 16 * HC * (s // 2 + 1))
            for d in range(DC):
                ps = p.pp[it % 2]
                if it >= 2:
                    tensor.wait_ge(p.s_ev.h, ev0 + it - 1)
                for c2 in range(HC):
                    tensor.matmul(ps[:, :], Cb_[:, c2, bass.ts(d, 128)],
                                  hsl[:, c2, :], start=(c2 == 0), stop=False)
                for d1 in range(DC):
                    mm = tensor.matmul(ps[:, :], Db_[:, d1, bass.ts(d, 128)],
                                       x1[:, d1, bass.ts(s, SW)],
                                       start=False, stop=(d1 == DC - 1))
                mm.then_inc(p.s_mm.h, 1)
                it += 1

    def ph5_scalar(scalar):
        it = 0
        for s in range(SC):
            for d in range(DC):
                scalar.wait_ge(p.s_mm.h, mm0 + it + 1)
                scalar.activation(ssm[:, d, bass.ts(s, SW)], p.pp[it % 2][:, :],
                                  AF.Identity, bias=p.colb[:, d:d + 1]) \
                    .then_inc(p.s_ev.h, 1)
                it += 1

    run_block(nc, ("gpsimd", "tensor", "scalar"),
              {"gpsimd": ph5_gp, "tensor": ph5_tensor, "scalar": ph5_scalar})
    p.s_gd[0].n = gd0[0] + 16 * HC * (SC // 2)
    p.s_gd[1].n = gd0[1] + 16 * HC * (SC // 2)
    p.s_mm.n = mm0 + SC * DC
    p.s_ev.n = ev0 + SC * DC
    p.barrier()
    if dbg:
        emit_dump(p, dbg["ssm"], ssm)

    w2b = p.bf3(RS, DC, D)
    emit_load_bf(p, [(Wb["w2T"][k], w2b[:, k, :]) for k in range(DC)])
    xres = p.bf3(BIG0, DC, S)
    emit_proj(
        p, n_o=DC, n_k=DC,
        lhsT=lambda o, k: w2b[:, k, bass.ts(o, 128)],
        rhs=lambda k, s: ssm[:, k, bass.ts(s, SW)],
        evac_cb=lambda eng, o, s, ps: eng.activation(
            xres[:, o, bass.ts(s, SW)], ps[:, :], AF.Silu),
    )
    ve0 = p.s_ve.n

    def z_vec(vector):
        for d in range(DC):
            vector.tensor_tensor(xres[:, d, :], ssm[:, d, :], xres[:, d, :],
                                 ALU.mult)
        vector.nop().then_inc(p.s_ve.h, 1)

    run_block(nc, ("vector",), {"vector": z_vec})
    p.s_ve.n = ve0 + 1
    p.barrier()

    w3b = p.bf3(RS, DC, D)
    emit_load_bf(p, [(Wb["w3T"][k], w3b[:, k, :]) for k in range(DC)])
    out0 = ssm
    emit_proj(
        p, n_o=DC, n_k=DC,
        lhsT=lambda o, k: w3b[:, k, bass.ts(o, 128)],
        rhs=lambda k, s: xres[:, k, bass.ts(s, SW)],
        evac_cb=lambda eng, o, s, ps: eng.tensor_add(
            out0[:, o, bass.ts(s, SW)], ps[:, :], ssm[:, o, bass.ts(s, SW)]),
        evac_engine="vector",
    )
    if dbg:
        emit_dump(p, dbg["out0"], out0)

    emit_post_ln_residual(p, out0=out0, osq_off=BIG0, xT_in=xT_in, xT_out=xT_out)


def emit_post_ln_residual(p, *, out0, osq_off, xT_in, xT_out):
    nc = p.nc
    osq = p.bf3(osq_off, DC, S)
    sc0 = p.s_sc.n

    def sq_sc(scalar):
        for d in range(DC):
            scalar.activation(osq[:, d, :], out0[:, d, :], AF.Square)
        scalar.nop().then_inc(p.s_sc.h, 1)

    run_block(nc, ("scalar",), {"scalar": sq_sc})
    p.s_sc.n = sc0 + 1
    p.barrier()

    m_b = p.f32(RS, 4096)
    r_b = p.f32(RS + 4096, 4096)
    sx2_b = p.f32(RS + 8192, 4096)
    emit_stat_pass(p, out0, m_b)
    emit_stat_pass(p, osq, sx2_b)
    emit_var_rstd(p, m_b, sx2_b, r_b)

    xs0 = p.f32(osq_off, 4096)
    xs1 = p.f32(osq_off + 4096, 4096)
    ot0 = p.f32(osq_off + 8192, 4096)
    ot1 = p.f32(osq_off + 12288, 4096)
    ld0 = [p.s_ld[0].n, p.s_ld[1].n]
    gd0 = [p.s_gd[0].n, p.s_gd[1].n]
    ve0 = p.s_ve.n

    def res_sync(sync):
        for d in range(DC):
            if d >= 2:
                sync.wait_ge(p.s_ve.h, ve0 + 3 * (d - 1))
            sync.dma_start((xs0, xs1)[d % 2], xT_in[d]) \
                .then_inc(p.s_ld[d % 2].h, 16)

    def res_vec(vector):
        for d in range(DC):
            vector.wait_ge(p.s_ld[d % 2].h, ld0[d % 2] + 16 * (d // 2 + 1))
            if d >= 2:
                vector.wait_ge(p.s_gd[d % 2].h, gd0[d % 2] + 16 * (d // 2))
            tt = (ot0, ot1)[d % 2]
            vector.tensor_sub(tt[:, :], out0[:, d, :], m_b[:, :])
            vector.tensor_mul(tt[:, :], tt[:, :], r_b[:, :])
            vector.tensor_add(tt[:, :], (xs0, xs1)[d % 2], tt[:, :]) \
                .then_inc(p.s_ve.h, 3)

    def res_gp(gpsimd):
        for d in range(DC):
            gpsimd.wait_ge(p.s_ve.h, ve0 + 3 * (d + 1))
            gpsimd.dma_start(xT_out[d], (ot0, ot1)[d % 2][:, :]) \
                .then_inc(p.s_gd[d % 2].h, 16)
        gpsimd.wait_ge(p.s_gd[0].h, gd0[0] + 16 * (DC // 2))
        gpsimd.wait_ge(p.s_gd[1].h, gd0[1] + 16 * (DC // 2))

    run_block(nc, ("sync", "vector", "gpsimd"),
              {"sync": res_sync, "vector": res_vec, "gpsimd": res_gp})
    p.s_ld[0].n = ld0[0] + 16 * (DC // 2)
    p.s_ld[1].n = ld0[1] + 16 * (DC // 2)
    p.s_ve.n = ve0 + 3 * DC
    p.s_gd[0].n = gd0[0] + 16 * (DC // 2)
    p.s_gd[1].n = gd0[1] + 16 * (DC // 2)
    p.barrier()



def emit_block3(p, xT_in, xT_out, W, Wb, dbg=None):
    nc = p.nc

    sd0 = p.s_ld[0]
    ld0 = sd0.n

    def init_gp(gpsimd):
        gpsimd.memset(p.ones128[:, :], 1.0)
        gpsimd.memset(p.cst[:, 0:1], LNEPS)
        gpsimd.memset(p.cst[:, 1:2], 1.0)
        gpsimd.memset(p.cst[:, 2:3], 1e-30)

    def init_sync(sync):
        sync.dma_start(p.small[0:64, 0:50], W["inner"]).then_inc(sd0.h, 16)
        sync.wait_ge(sd0.h, ld0 + 16)

    run_block(nc, ("gpsimd", "sync"), {"gpsimd": init_gp, "sync": init_sync})
    sd0.n = ld0 + 16
    p.barrier()

    xn = p.bf3(BIG0, DC, S)
    xs0 = p.f32(BIG1, 4096)
    xs1 = p.f32(BIG1 + 4096, 4096)
    sx2_b = p.f32(BIG1 + 8192, 4096)
    t_ap = p.f32(BIG1 + 12288, 4096)
    m_b = p.f32(RS, 4096)
    r_b = p.f32(RS + 4096, 4096)
    emit_stream_stats(p, xT_in, xn, m_b, r_b, sx2_b, xs0, xs1)
    emit_apply_ln(p, xT_in, xn, m_b, r_b, t_ap, xs0, xs1)

    winb = p.bf3(RS + 12288, DC, 64)
    emit_load_bf(p, [(Wb["WinT"][k], winb[:, k, :]) for k in range(DC)])
    sT = p.f32(RS + 8192, 4096)
    mm0, ev0 = p.s_mm.n, p.s_ev.n

    def win_tensor(tensor):
        for s in range(SC):
            ps = p.pp[s % 2]
            if s >= 2:
                tensor.wait_ge(p.s_ev.h, ev0 + s - 1)
            for k in range(DC):
                mm = tensor.matmul(ps[0:64, :], winb[:, k, :],
                                   xn[:, k, bass.ts(s, SW)],
                                   start=(k == 0), stop=(k == DC - 1))
            mm.then_inc(p.s_mm.h, 1)

    def win_evac(scalar):
        for s in range(SC):
            scalar.wait_ge(p.s_mm.h, mm0 + s + 1)
            scalar.activation(sT[0:64, bass.ts(s, SW)], p.pp[s % 2][0:64, :],
                              AF.Identity, bias=p.small[0:64, 48:49]) \
                .then_inc(p.s_ev.h, 1)

    run_block(nc, ("tensor", "scalar"), {"tensor": win_tensor, "scalar": win_evac})
    p.s_mm.n = mm0 + SC
    p.s_ev.n = ev0 + SC
    p.barrier()

    g = p.f32(BIG1, 4096)
    tj = p.f32(BIG1 + 4096, 4096)
    sw = p.small
    ve0 = p.s_ve.n

    def inner_vec(vector):
        for j in range(16):
            vector.tensor_scalar(tj[0:64, :], sT[0:64, :], sw[0:64, j:j + 1],
                                 sw[0:64, 16 + j:17 + j], ALU.mult, ALU.add)
            vector.tensor_scalar_max(tj[0:64, :], tj[0:64, :], 0.0)
            if j == 0:
                vector.tensor_scalar_mul(g[0:64, :], tj[0:64, :],
                                         sw[0:64, 32 + j:33 + j])
            else:
                vector.scalar_tensor_tensor(g[0:64, :], tj[0:64, :],
                                            sw[0:64, 32 + j:33 + j], g[0:64, :],
                                            ALU.mult, ALU.add)
        vector.tensor_scalar_add(g[0:64, :], g[0:64, :], sw[0:64, 49:50])
        vector.tensor_copy(gbf[0:64, :], g[0:64, :]).then_inc(p.s_ve.h, 1)

    gbf = p.bf(BIG1 + 8192, 4096)
    run_block(nc, ("vector",), {"vector": inner_vec})
    p.s_ve.n = ve0 + 1
    p.barrier()

    woutb = p.bf(RS + 12288, 2048)
    woutb = p.bf3(RS + 12288, 1, 1024)
    emit_load_bf(p, [(Wb["Wout64"], woutb[0:64, 0, :])])
    boutc = p.colb[:, 0:8]
    sdl = p.s_ld[0]
    ldb = sdl.n

    def bout_sync(sync):
        sync.dma_start(boutc, W["bout8"]).then_inc(sdl.h, 16)
        sync.wait_ge(sdl.h, ldb + 16)

    run_block(nc, ("sync",), {"sync": bout_sync})
    sdl.n = ldb + 16
    p.barrier()

    out0 = p.bf3(BIG0, DC, S)
    mm0, ev0 = p.s_mm.n, p.s_ev.n

    def wout_tensor(tensor):
        it = 0
        for o in range(DC):
            for s in range(SC):
                ps = p.pp[it % 2]
                if it >= 2:
                    tensor.wait_ge(p.s_ev.h, ev0 + it - 1)
                mm = tensor.matmul(ps[:, :], woutb[0:64, 0, bass.ts(o, 128)],
                                   gbf[0:64, bass.ts(s, SW)],
                                   start=True, stop=True)
                mm.then_inc(p.s_mm.h, 1)
                it += 1

    def wout_evac(scalar):
        it = 0
        for o in range(DC):
            for s in range(SC):
                scalar.wait_ge(p.s_mm.h, mm0 + it + 1)
                scalar.activation(out0[:, o, bass.ts(s, SW)], p.pp[it % 2][:, :],
                                  AF.Identity, bias=p.colb[:, o:o + 1]) \
                    .then_inc(p.s_ev.h, 1)
                it += 1

    run_block(nc, ("tensor", "scalar"), {"tensor": wout_tensor, "scalar": wout_evac})
    p.s_mm.n = mm0 + DC * SC
    p.s_ev.n = ev0 + DC * SC
    p.barrier()
    if dbg:
        emit_dump(p, dbg["f3"], out0)

    emit_post_ln_residual(p, out0=out0, osq_off=BIG1, xT_in=xT_in, xT_out=xT_out)


def host_block3(inputs):
    Wh = {}
    Wh["b3_WinT"] = np.ascontiguousarray(
        inputs["L1_ffn_Win"].reshape(DC, 128, 64))
    inner = np.concatenate([
        inputs["L1_ffn_W1"], inputs["L1_ffn_b1"], inputs["L1_ffn_W2"],
        inputs["L1_ffn_bin_"].reshape(64, 1), inputs["L1_ffn_b2"].reshape(64, 1),
    ], axis=1)
    Wh["b3_inner"] = np.ascontiguousarray(inner)
    Wh["b3_Wout"] = np.ascontiguousarray(inputs["L1_ffn_Wout"])
    Wh["b3_bout"] = np.ascontiguousarray(
        inputs["L1_ffn_bout"].reshape(DC, 128).T)
    return Wh


def declare_block3(nc):
    t = {}
    t["WinT"] = nc.dram_tensor("b3_WinT", [DC, 128, 64], F32, kind="ExternalInput")
    t["inner"] = nc.dram_tensor("b3_inner", [64, 50], F32, kind="ExternalInput")
    t["Wout"] = nc.dram_tensor("b3_Wout", [64, 1024], F32, kind="ExternalInput")
    t["bout"] = nc.dram_tensor("b3_bout", [128, 8], F32, kind="ExternalInput")
    tb = {"WinT": mirror_tensor(nc, t["WinT"]),
          "Wout": mirror_tensor(nc, t["Wout"])}
    W = {}
    W["WinT"] = [t["WinT"][k, :, :] for k in range(DC)]
    W["inner"] = t["inner"][:, :]
    W["Wout64"] = t["Wout"][:, :]
    W["bout8"] = t["bout"][:, :]
    Wb = {}
    Wb["WinT"] = [tb["WinT"][k, :, :] for k in range(DC)]
    Wb["Wout64"] = tb["Wout"][:, :]
    mp = [(t["WinT"][k, :, :], tb["WinT"][k, :, :]) for k in range(DC)]
    mp.append((t["Wout"][:, :], tb["Wout"][:, :]))
    return W, Wb, mp


def build_block3_program(debug=False):
    nc = bass.Bass("TRN2", target_bir_lowering=False)
    xT_in_t = nc.dram_tensor("xT_in", [DC, 128, S], F32, kind="ExternalInput")
    xT_out_t = nc.dram_tensor("xT_out", [DC, 128, S], F32, kind="ExternalOutput")
    W, Wb, mp = declare_block3(nc)
    dbg = None
    if debug:
        t = nc.dram_tensor("dbg_f3", [DC, 128, S], BF16, kind="ExternalOutput")
        dbg = {"f3": [t[d, :, :] for d in range(DC)]}
    with ExitStack() as ctx:
        p = Prog(nc, ctx)
        emit_mirror_prelude(p, mp, RS)
        emit_block3(p, [xT_in_t[d, :, :] for d in range(DC)],
                    [xT_out_t[d, :, :] for d in range(DC)], W, Wb, dbg=dbg)
    return nc



def emit_dump_flat(p, dram_ap, buf):
    sg = p.s_gd[0]
    g0 = sg.n

    def gp(gpsimd):
        gpsimd.dma_start(dram_ap, buf).then_inc(sg.h, 16)
        gpsimd.wait_ge(sg.h, g0 + 16)

    run_block(p.nc, ("gpsimd",), {"gpsimd": gp})
    sg.n = g0 + 16
    p.barrier()


def emit_block1(p, xT_in, xT_out, W, Wb, v_spill, outT_spill, dbg=None):
    nc = p.nc
    NPAIR = 8
    NKT = 32
    GRP = 4

    def init_gp(gpsimd):
        gpsimd.memset(p.ones128[:, :], 1.0)
        gpsimd.memset(p.cst[:, 0:1], LNEPS)
        gpsimd.memset(p.cst[:, 1:2], 1.0)
        gpsimd.memset(p.cst[:, 2:3], 1e-30)

    run_block(nc, ("gpsimd",), {"gpsimd": init_gp})
    p.barrier()

    xn = p.bf3(BIG0, DC, S)
    xs0 = p.f32(BIG1, 4096)
    xs1 = p.f32(BIG1 + 4096, 4096)
    sx2_b = p.f32(BIG1 + 8192, 4096)
    t_ap = p.f32(BIG1 + 12288, 4096)
    m_b = p.f32(RS, 4096)
    r_b = p.f32(RS + 4096, 4096)
    with nc.named_scope("b1_ln"):
        emit_stream_stats(p, xT_in, xn, m_b, r_b, sx2_b, xs0, xs1)
        emit_apply_ln(p, xT_in, xn, m_b, r_b, t_ap, xs0, xs1)

    wv = p.bf3(RS, DC, D)
    emit_load_bf(p, [(Wb["WvR"][k], wv[:, k, :]) for k in range(DC)])
    vt0 = p.bf(RS + 4096, 512)
    vt1 = p.bf(RS + 4352, 512)
    mm0, ev0 = p.s_mm.n, p.s_ev.n
    ad0 = [p.s_ad[0].n, p.s_ad[1].n]

    def vb_tensor(tensor):
        it = 0
        for t in range(32):
            for h in range(2):
                ps = p.pp[it % 2]
                if it >= 2:
                    tensor.wait_ge(p.s_ev.h, ev0 + it - 1)
                for k in range(DC):
                    mm = tensor.matmul(ps[:, :], xn[:, k, bass.ts(t, 128)],
                                       wv[:, k, bass.ts(h, 512)],
                                       start=(k == 0), stop=(k == DC - 1))
                mm.then_inc(p.s_mm.h, 1)
                it += 1

    def vb_scalar(scalar):
        it = 0
        for t in range(32):
            for h in range(2):
                scalar.wait_ge(p.s_mm.h, mm0 + it + 1)
                if it >= 2:
                    scalar.wait_ge(p.s_ad[it % 2].h,
                                   ad0[it % 2] + 16 * 4 * (it // 2))
                vt = (vt0, vt1)[it % 2]
                scalar.activation(vt[:, :], p.pp[it % 2][:, :], AF.Identity) \
                    .then_inc(p.s_ev.h, 1)
                scalar.wait_ge(p.s_ev.h, ev0 + it + 1)
                for pj in range(4):
                    pair = h * 4 + pj
                    scalar.dma_start(v_spill[pair][:, t, :],
                                     vt[:, bass.ts(pj, 128)]) \
                        .then_inc(p.s_ad[it % 2].h, 16)
                it += 1
        scalar.wait_ge(p.s_ad[0].h, ad0[0] + 16 * 4 * 32)
        scalar.wait_ge(p.s_ad[1].h, ad0[1] + 16 * 4 * 32)

    run_block(nc, ("tensor", "scalar"),
              {"tensor": vb_tensor, "scalar": vb_scalar})
    p.s_mm.n = mm0 + 64
    p.s_ev.n = ev0 + 64
    p.s_ad[0].n = ad0[0] + 16 * 4 * 32
    p.s_ad[1].n = ad0[1] + 16 * 4 * 32
    p.barrier()

    wqk = p.bf3(RS, 16, 128)
    qT = p.bf(RS + 1024, 4096)
    kT = p.bf(RS + 3072, 4096)
    vp = p.bf3(RS + 5120, 32, 128)
    acc = p.f32(RS + 7168, 4096)
    sq = p.bf(RS + 11264, 4096)
    vpr = p.bf3(RS + 13312, GRP, 128)
    E = p.bf3(BIG1, 2 * GRP, S)
    nrm = p.small
    zb1 = p.bf(RS + 13568, 2048)
    zb2 = p.f32(RS + 14592, 1024)

    for pair in range(NPAIR):
        emit_load_bf(p, [(Wb["WqP"][pair],
                          wqk[:, 0:8, :].rearrange("p a b -> p (a b)")),
                         (Wb["WkP"][pair],
                          wqk[:, 8:16, :].rearrange("p a b -> p (a b)"))])
        sg = p.s_gd[0]
        g0 = sg.n

        def vp_gp(gpsimd, pair=pair):
            gpsimd.dma_start(vp[:, :, :], v_spill[pair][:, :, :]).then_inc(sg.h, 16)
            gpsimd.wait_ge(sg.h, g0 + 16)

        run_block(nc, ("gpsimd",), {"gpsimd": vp_gp})
        sg.n = g0 + 16
        p.barrier()

        sid = nc.enter_named_scope("b1_c0123", False)[0]
        sidq = nc.enter_named_scope("b1_c12", False)[0]
        emit_proj(p, n_o=1, n_k=DC,
                  lhsT=lambda o, k: wqk[:, k, :],
                  rhs=lambda k, s: xn[:, k, bass.ts(s, SW)],
                  evac_cb=lambda eng, o, s, ps: eng.activation(
                      qT[:, bass.ts(s, SW)], ps[:, :], AF.Identity,
                      scale=1.0 / 32.0))
        emit_proj(p, n_o=1, n_k=DC,
                  lhsT=lambda o, k: wqk[:, 8 + k, :],
                  rhs=lambda k, s: xn[:, k, bass.ts(s, SW)],
                  evac_cb=lambda eng, o, s, ps: eng.activation(
                      kT[:, bass.ts(s, SW)], ps[:, :], AF.Identity))

        nc.leave_named_scope("b1_c12", sidq, False)
        sidn = nc.enter_named_scope("b1_c3", False)[0]
        sc0 = p.s_sc.n

        def sq_q(scalar):
            scalar.activation(sq[:, :], qT[:, :], AF.Square).then_inc(p.s_sc.h, 1)

        run_block(nc, ("scalar",), {"scalar": sq_q})
        p.s_sc.n = sc0 + 1
        p.barrier()
        emit_norm_max(p, sq, nrm, 0)
        sc0 = p.s_sc.n

        def sq_k(scalar):
            scalar.activation(sq[:, :], kT[:, :], AF.Square).then_inc(p.s_sc.h, 1)

        run_block(nc, ("scalar",), {"scalar": sq_k})
        p.s_sc.n = sc0 + 1
        p.barrier()
        emit_norm_max(p, sq, nrm, 2)

        ve0, sc0 = p.s_ve.n, p.s_sc.n

        def mM_vec(vector):
            vector.tensor_tensor(nrm[:, 36:38], nrm[:, 32:34], nrm[:, 34:36],
                                 ALU.mult).then_inc(p.s_ve.h, 1)

        def mM_sc(scalar):
            scalar.wait_ge(p.s_ve.h, ve0 + 1)
            scalar.activation(nrm[:, 38:40], nrm[:, 36:38], AF.Ln, bias=1e-30)
            scalar.activation(nrm[:, 38:40], nrm[:, 38:40], AF.Exp, scale=0.5)
            scalar.mul(p.colb[:, 8:10], nrm[:, 38:40], -1.0).then_inc(p.s_sc.h, 1)

        run_block(nc, ("vector", "scalar"), {"vector": mM_vec, "scalar": mM_sc})
        p.s_ve.n = ve0 + 1
        p.s_sc.n = sc0 + 1
        p.barrier()

        if dbg and pair == 0:
            emit_dump_flat(p, dbg["vp"],
                           vp[:, :, :].rearrange("p a b -> p (a b)"))
            emit_dump_flat(p, dbg["qT"], qT[:, :])
            emit_dump_flat(p, dbg["kT"], kT[:, :])
            emit_dump_flat(p, dbg["Mcol"], p.colb[:, 0:16])
            emit_dump_flat(p, dbg["nrm"], nrm[:, :])
        sq3 = sq.rearrange("p (a b) -> p a b", b=S)
        nc.leave_named_scope("b1_c3", sidn, False)
        nc.leave_named_scope("b1_c0123", sid, False)
        sid2 = nc.enter_named_scope("b1_c4", False)[0]
        for g in range(NKT // GRP):
            mm0, ev0, ve0 = p.s_mm.n, p.s_ev.n, p.s_ve.n
            nit = 0

            def s_tensor(tensor, g=g, mm0=mm0, ev0=ev0, ve0=ve0):
                it = 0
                for j in range(GRP):
                    kt = g * GRP + j
                    for s in range(SC):
                        psA = p.pp[2 * (it % 2)]
                        psB = p.pp[2 * (it % 2) + 1]
                        if it >= 2:
                            tensor.wait_ge(p.s_ev.h, ev0 + 2 * (it - 1))
                        tensor.matmul(psA[:, :], kT[0:64, bass.ts(kt, 128)],
                                      qT[0:64, bass.ts(s, SW)],
                                      start=True, stop=True).then_inc(p.s_mm.h, 1)
                        tensor.matmul(psB[:, :], kT[64:128, bass.ts(kt, 128)],
                                      qT[64:128, bass.ts(s, SW)],
                                      start=True, stop=True).then_inc(p.s_mm.h, 1)
                        it += 1

            def s_scalar(scalar, g=g, mm0=mm0, ev0=ev0):
                it = 0
                for j in range(GRP):
                    for s in range(SC):
                        scalar.wait_ge(p.s_mm.h, mm0 + 2 * it + 2)
                        psA = p.pp[2 * (it % 2)]
                        psB = p.pp[2 * (it % 2) + 1]
                        scalar.activation(E[:, 2 * j, bass.ts(s, SW)], psA[:, :],
                                          AF.Exp, bias=p.colb[:, 8:9]) \
                            .then_inc(p.s_ev.h, 1)
                        scalar.activation(E[:, 2 * j + 1, bass.ts(s, SW)], psB[:, :],
                                          AF.Exp, bias=p.colb[:, 9:10]) \
                            .then_inc(p.s_ev.h, 1)
                        it += 1

            def s_vector(vector, g=g, ev0=ev0, ve0=ve0):
                for j in range(GRP):
                    kt = g * GRP + j
                    vector.wait_ge(p.s_ev.h, ev0 + 16 * (j + 1))
                    for h in range(2):
                        Ejh = E[:, 2 * j + h, :]
                        vector.tensor_add(zb1[:, :], Ejh[:, 0:2048],
                                          Ejh[:, 2048:4096])
                        vector.tensor_add(zb2[:, :], zb1[:, 0:1024],
                                          zb1[:, 1024:2048])
                        vector.reduce_sum(nrm[:, 16 + h:17 + h], zb2[:, :],
                                          mybir.AxisListType.X) \
                            .then_inc(p.s_ve.h, 1)
                    vector.wait_ge(p.s_ve.h, ve0 + 5 * j + 2)
                    vector.reciprocal(nrm[:, 18:20], nrm[:, 16:18]) \
                        .then_inc(p.s_ve.h, 1)
                    vector.wait_ge(p.s_ve.h, ve0 + 5 * j + 3)
                    vector.tensor_tensor(vpr[:, j, 0:64], vp[:, kt, 0:64],
                                         nrm[:, 18:19].broadcast_to((128, 64)),
                                         ALU.mult).then_inc(p.s_ve.h, 1)
                    vector.tensor_tensor(vpr[:, j, 64:128], vp[:, kt, 64:128],
                                         nrm[:, 19:20].broadcast_to((128, 64)),
                                         ALU.mult).then_inc(p.s_ve.h, 1)

            run_block(nc, ("tensor", "scalar", "vector"),
                      {"tensor": s_tensor, "scalar": s_scalar,
                       "vector": s_vector})
            p.s_mm.n = mm0 + 2 * GRP * SC
            p.s_ev.n = ev0 + 2 * GRP * SC
            p.s_ve.n = ve0 + 5 * GRP
            p.barrier()
            if dbg and pair == 0 and g == 0:
                emit_dump_flat(p, dbg["nrm2"], nrm[:, :])
                emit_dump_flat(p, dbg["vpr"],
                               vpr[:, :, :].rearrange("p a b -> p (a b)"))
                emit_dump_flat(p, dbg["E00"], E[:, 0, :])

            mm0, ve0 = p.s_mm.n, p.s_ve.n

            def ev_tensor(tensor, g=g, mm0=mm0, ve0=ve0):
                for qc in range(SC):
                    ps = p.pp[qc % 2]
                    if qc >= 2:
                        tensor.wait_ge(p.s_ve.h, ve0 + qc - 1)
                    for j in range(GRP):
                        tensor.matmul(ps[0:64, :], vpr[:, j, 0:64],
                                      E[:, 2 * j, bass.ts(qc, SW)],
                                      start=(j == 0), stop=(j == GRP - 1),
                                      skip_group_check=True)
                        mm = tensor.matmul(ps[64:128, :], vpr[:, j, 64:128],
                                           E[:, 2 * j + 1, bass.ts(qc, SW)],
                                           start=(j == 0), stop=(j == GRP - 1),
                                           tile_position=(0, 64),
                                           skip_group_check=True)
                    mm.then_inc(p.s_mm.h, 1)

            def ev_vector(vector, g=g, mm0=mm0):
                for qc in range(SC):
                    vector.wait_ge(p.s_mm.h, mm0 + qc + 1)
                    ps = p.pp[qc % 2]
                    if g == 0:
                        vector.tensor_copy(acc[:, bass.ts(qc, SW)], ps[:, :]) \
                            .then_inc(p.s_ve.h, 1)
                    else:
                        vector.tensor_tensor(acc[:, bass.ts(qc, SW)], ps[:, :],
                                             acc[:, bass.ts(qc, SW)], ALU.add) \
                            .then_inc(p.s_ve.h, 1)

            run_block(nc, ("tensor", "vector"),
                      {"tensor": ev_tensor, "vector": ev_vector})
            p.s_mm.n = mm0 + SC
            p.s_ve.n = ve0 + SC
            p.barrier()
            if dbg and pair == 0 and g == 0:
                emit_dump_flat(p, dbg["acc0"], acc[:, :])

        nc.leave_named_scope("b1_c4", sid2, False)
        ve0 = p.s_ve.n
        sg = p.s_gd[0]
        g0 = sg.n

        if dbg and pair == 0:
            emit_dump_flat(p, dbg["acc"], acc[:, :])
            emit_dump_flat(p, dbg["E0"], E[:, 0, :])

        def sp_vec(vector):
            vector.tensor_copy(sq[:, :], acc[:, :]).then_inc(p.s_ve.h, 1)

        def sp_gp(gpsimd, pair=pair):
            gpsimd.wait_ge(p.s_ve.h, ve0 + 1)
            gpsimd.dma_start(outT_spill[pair], sq[:, :]).then_inc(sg.h, 16)
            gpsimd.wait_ge(sg.h, g0 + 16)

        run_block(nc, ("vector", "gpsimd"), {"vector": sp_vec, "gpsimd": sp_gp})
        p.s_ve.n = ve0 + 1
        sg.n = g0 + 16
        p.barrier()

    zT = p.bf3(BIG1, DC, S)
    sg = p.s_gd[0]
    g0 = sg.n

    def rl_gp(gpsimd):
        for pr in range(NPAIR):
            gpsimd.dma_start(zT[:, pr, :], outT_spill[pr]).then_inc(sg.h, 16)
        gpsimd.wait_ge(sg.h, g0 + 16 * NPAIR)

    run_block(nc, ("gpsimd",), {"gpsimd": rl_gp})
    sg.n = g0 + 16 * NPAIR
    p.barrier()

    if dbg:
        emit_dump(p, dbg["zT"], zT)
    wo = p.bf3(RS, DC, D)
    emit_load_bf(p, [(Wb["WoR"][k], wo[:, k, :]) for k in range(DC)])
    z1 = p.bf3(BIG0, DC, S)
    emit_proj(p, n_o=DC, n_k=DC,
              lhsT=lambda o, k: wo[:, k, bass.ts(o, 128)],
              rhs=lambda k, s: zT[:, k, bass.ts(s, SW)],
              evac_cb=lambda eng, o, s, ps: eng.activation(
                  z1[:, o, bass.ts(s, SW)], ps[:, :], AF.Identity))

    if dbg:
        emit_dump(p, dbg["z1"], z1)
    emit_l2norm(p, z1, p.bf3(BIG1, DC, S), sqbig_off=BIG1)

    zn1 = p.bf3(BIG1, DC, S)
    wff = p.bf3(RS, DC, D)
    emit_load_bf(p, [(Wb["WffR"][k], wff[:, k, :]) for k in range(DC)])
    z2 = p.bf3(BIG0, DC, S)
    emit_proj(p, n_o=DC, n_k=DC,
              lhsT=lambda o, k: wff[:, k, bass.ts(o, 128)],
              rhs=lambda k, s: zn1[:, k, bass.ts(s, SW)],
              evac_cb=lambda eng, o, s, ps: eng.activation(
                  z2[:, o, bass.ts(s, SW)], ps[:, :], AF.Identity))

    emit_l2norm(p, z2, p.bf3(BIG1, DC, S), sqbig_off=BIG1, gelu=True)
    out0 = p.bf3(BIG1, DC, S)
    if dbg:
        emit_dump(p, dbg["f1"], out0)

    emit_post_ln_residual(p, out0=out0, osq_off=BIG0, xT_in=xT_in, xT_out=xT_out)


def emit_norm_max(p, sqbuf, nrm, slot):
    nc = p.nc
    mm0, ve0 = p.s_mm.n, p.s_ve.n
    sq3 = sqbuf.rearrange("p (a b) -> p a b", b=S)

    def nt(tensor):
        for s in range(SC):
            psA = p.pp[2 * (s % 2)]
            psB = p.pp[2 * (s % 2) + 1]
            if s >= 2:
                tensor.wait_ge(p.s_ve.h, ve0 + 2 * (s - 1))
            tensor.matmul(psA[:, :], p.ones128[0:64, :],
                          sq3[0:64, 0, bass.ts(s, SW)], start=True, stop=True) \
                .then_inc(p.s_mm.h, 1)
            tensor.matmul(psB[:, :], p.ones128[64:128, :],
                          sq3[64:128, 0, bass.ts(s, SW)], start=True, stop=True) \
                .then_inc(p.s_mm.h, 1)

    def nv(vector):
        for s in range(SC):
            vector.wait_ge(p.s_mm.h, mm0 + 2 * s + 2)
            vector.reduce_max(nrm[:, s:s + 1], p.pp[2 * (s % 2)][:, :],
                              mybir.AxisListType.X)
            vector.reduce_max(nrm[:, 8 + s:9 + s], p.pp[2 * (s % 2) + 1][:, :],
                              mybir.AxisListType.X).then_inc(p.s_ve.h, 2)
        vector.wait_ge(p.s_ve.h, ve0 + 2 * SC)
        vector.reduce_max(nrm[:, 32 + slot:33 + slot], nrm[:, 0:8],
                          mybir.AxisListType.X)
        vector.reduce_max(nrm[:, 33 + slot:34 + slot], nrm[:, 8:16],
                          mybir.AxisListType.X).then_inc(p.s_ve.h, 1)

    run_block(nc, ("tensor", "vector"), {"tensor": nt, "vector": nv})
    p.s_mm.n = mm0 + 2 * SC
    p.s_ve.n = ve0 + 2 * SC + 1
    p.barrier()


def emit_l2norm(p, zin, zout, sqbig_off, gelu=False):
    nc = p.nc
    osq = p.bf3(sqbig_off, DC, S)
    sc0 = p.s_sc.n

    def sq_sc(scalar):
        for d in range(DC):
            scalar.activation(osq[:, d, :], zin[:, d, :], AF.Square)
        scalar.nop().then_inc(p.s_sc.h, 1)

    run_block(nc, ("scalar",), {"scalar": sq_sc})
    p.s_sc.n = sc0 + 1
    p.barrier()

    s2 = p.f32(RS + 8192, 4096)
    r_b = p.f32(RS + 12288, 4096)
    emit_stat_pass(p, osq, s2, inv_n=1.0)
    ve0, sc0 = p.s_ve.n, p.s_sc.n

    def rn_sc(scalar):
        scalar.activation(r_b[:, :], s2[:, :], AF.Ln, bias=1e-30)
        scalar.activation(r_b[:, :], r_b[:, :], AF.Exp, scale=0.5) \
            .then_inc(p.s_sc.h, 1)

    def rn_vec(vector):
        vector.wait_ge(p.s_sc.h, sc0 + 1)
        vector.tensor_scalar_max(r_b[:, :], r_b[:, :], 1e-12) \
            .then_inc(p.s_ve.h, 1)
        vector.wait_ge(p.s_ve.h, ve0 + 1)
        vector.reciprocal(r_b[:, :], r_b[:, :]).then_inc(p.s_ve.h, 1)

    run_block(nc, ("scalar", "vector"), {"scalar": rn_sc, "vector": rn_vec})
    p.s_sc.n = sc0 + 1
    p.s_ve.n = ve0 + 2
    p.barrier()

    t_g = p.f32(RS + 8192, 4096)
    ve0, sc0 = p.s_ve.n, p.s_sc.n

    def ap_vec(vector):
        for d in range(DC):
            if gelu:
                if d >= 1:
                    vector.wait_ge(p.s_sc.h, sc0 + d)
                vector.tensor_mul(t_g[:, :], zin[:, d, :], r_b[:, :]) \
                    .then_inc(p.s_ve.h, 1)
            else:
                vector.tensor_mul(zout[:, d, :], zin[:, d, :], r_b[:, :]) \
                    .then_inc(p.s_ve.h, 1)

    def ap_sc(scalar):
        for d in range(DC):
            scalar.wait_ge(p.s_ve.h, ve0 + d + 1)
            scalar.activation(zout[:, d, :], t_g[:, :], AF.Gelu) \
                .then_inc(p.s_sc.h, 1)

    bodies = {"vector": ap_vec}
    if gelu:
        bodies["scalar"] = ap_sc
    run_block(nc, ("vector", "scalar"), bodies)
    p.s_ve.n = ve0 + DC
    if gelu:
        p.s_sc.n = sc0 + DC
    p.barrier()


def host_block1(inputs):
    Wh = {}
    Wh["b1_WvR"] = np.ascontiguousarray(
        inputs["L0_ffn_Wv"].reshape(DC, 128, D))
    Wh["b1_WoR"] = np.ascontiguousarray(
        inputs["L0_ffn_Wo"].reshape(DC, 128, D))
    Wh["b1_WffR"] = np.ascontiguousarray(
        inputs["L0_ffn_Wff"].reshape(DC, 128, D))
    for nm, key in (("b1_WqP", "L0_ffn_Wq"), ("b1_WkP", "L0_ffn_Wk")):
        w = inputs[key].reshape(DC, 128, NCORE, 128)
        Wh[nm] = np.ascontiguousarray(w.transpose(2, 1, 0, 3)
                                      .reshape(NCORE, 128, D))
    return Wh


def declare_block1(nc):
    t = {}
    t["WvR"] = nc.dram_tensor("b1_WvR", [DC, 128, D], F32, kind="ExternalInput")
    t["WoR"] = nc.dram_tensor("b1_WoR", [DC, 128, D], F32, kind="ExternalInput")
    t["WffR"] = nc.dram_tensor("b1_WffR", [DC, 128, D], F32, kind="ExternalInput")
    t["WqP"] = nc.dram_tensor("b1_WqP", [8, 128, D], F32, kind="ExternalInput")
    t["WkP"] = nc.dram_tensor("b1_WkP", [8, 128, D], F32, kind="ExternalInput")
    tb = {k: mirror_tensor(nc, t[k]) for k in t}
    W = {}
    W["WvR"] = [t["WvR"][k, :, :] for k in range(DC)]
    W["WoR"] = [t["WoR"][k, :, :] for k in range(DC)]
    W["WffR"] = [t["WffR"][k, :, :] for k in range(DC)]
    W["WqP"] = [t["WqP"][pr, :, :] for pr in range(8)]
    W["WkP"] = [t["WkP"][pr, :, :] for pr in range(8)]
    Wb = {}
    Wb["WvR"] = [tb["WvR"][k, :, :] for k in range(DC)]
    Wb["WoR"] = [tb["WoR"][k, :, :] for k in range(DC)]
    Wb["WffR"] = [tb["WffR"][k, :, :] for k in range(DC)]
    Wb["WqP"] = [tb["WqP"][pr, :, :] for pr in range(8)]
    Wb["WkP"] = [tb["WkP"][pr, :, :] for pr in range(8)]
    mp = []
    for nm in t:
        for k in range(t[nm].shape[0]):
            mp.append((t[nm][k, :, :], tb[nm][k, :, :]))
    return W, Wb, mp


def build_block1_program(debug=False):
    nc = bass.Bass("TRN2", target_bir_lowering=False)
    xT_in_t = nc.dram_tensor("xT_in", [DC, 128, S], F32, kind="ExternalInput")
    xT_out_t = nc.dram_tensor("xT_out", [DC, 128, S], F32, kind="ExternalOutput")
    v_sp = nc.dram_tensor("v_sp", [8, 128, 32, 128], BF16, kind="Internal")
    o_sp = nc.dram_tensor("o_sp", [8, 128, S], BF16, kind="Internal")
    W, Wb, mp = declare_block1(nc)
    dbg = None
    if debug:
        t = nc.dram_tensor("dbg_f1", [DC, 128, S], BF16, kind="ExternalOutput")
        dbg = {"f1": [t[d, :, :] for d in range(DC)]}
        for nm in ("zT", "z1"):
            tt = nc.dram_tensor("dbg_" + nm, [DC, 128, S], BF16,
                                kind="ExternalOutput")
            dbg[nm] = [tt[d, :, :] for d in range(DC)]
        dbg["qT"] = nc.dram_tensor("dbg_qT", [128, S], BF16,
                                   kind="ExternalOutput")[:, :]
        dbg["kT"] = nc.dram_tensor("dbg_kT", [128, S], BF16,
                                   kind="ExternalOutput")[:, :]
        dbg["acc"] = nc.dram_tensor("dbg_acc", [128, S], F32,
                                    kind="ExternalOutput")[:, :]
        dbg["E0"] = nc.dram_tensor("dbg_E0", [128, S], BF16,
                                   kind="ExternalOutput")[:, :]
        dbg["Mcol"] = nc.dram_tensor("dbg_Mcol", [128, 16], F32,
                                     kind="ExternalOutput")[:, :]
        dbg["nrm"] = nc.dram_tensor("dbg_nrm", [128, 64], F32,
                                    kind="ExternalOutput")[:, :]
        dbg["nrm2"] = nc.dram_tensor("dbg_nrm2", [128, 64], F32,
                                     kind="ExternalOutput")[:, :]
        dbg["vpr"] = nc.dram_tensor("dbg_vpr", [128, 512], BF16,
                                    kind="ExternalOutput")[:, :]
        dbg["E00"] = nc.dram_tensor("dbg_E00", [128, S], BF16,
                                    kind="ExternalOutput")[:, :]
        dbg["acc0"] = nc.dram_tensor("dbg_acc0", [128, S], F32,
                                     kind="ExternalOutput")[:, :]
        dbg["vp"] = nc.dram_tensor("dbg_vp", [128, S], BF16,
                                   kind="ExternalOutput")[:, :]
    with ExitStack() as ctx:
        p = Prog(nc, ctx)
        emit_mirror_prelude(p, mp, RS)
        emit_block1(p, [xT_in_t[d, :, :] for d in range(DC)],
                    [xT_out_t[d, :, :] for d in range(DC)], W, Wb,
                    [v_sp[pr, :, :, :] for pr in range(8)],
                    [o_sp[pr, :, :] for pr in range(8)], dbg=dbg)
    return nc


def emit_block2(p, xT_in, xT_out, W, Wb, v_spill, outT_spill, dbg=None):
    nc = p.nc
    NPAIR = 8
    NM = 11

    sd0 = p.s_ld[0]
    ld0 = sd0.n

    def init_gp(gpsimd):
        gpsimd.memset(p.ones128[:, :], 1.0)
        gpsimd.memset(p.cst[:, 0:1], LNEPS)
        gpsimd.memset(p.cst[:, 1:2], 1.0)
        gpsimd.memset(p.cst[:, 2:3], 1e-30)

    def init_sync(sync):
        sync.dma_start(idt[:, :], W["idt"]).then_inc(sd0.h, 16)
        sync.dma_start(beta_f[:, :], W["beta"]).then_inc(sd0.h, 16)
        sync.wait_ge(sd0.h, ld0 + 32)

    idt = p.arena[:, RS + 17952:RS + 18016].bitcast(BF16)
    beta_f = p.arena[0:1, RS + 18016:RS + 18192]
    run_block(nc, ("gpsimd", "sync"), {"gpsimd": init_gp, "sync": init_sync})
    sd0.n = ld0 + 32
    p.barrier()

    xn = p.bf3(BIG0, DC, S)
    xs0 = p.f32(BIG1, 4096)
    xs1 = p.f32(BIG1 + 4096, 4096)
    sx2_b = p.f32(BIG1 + 8192, 4096)
    t_ap = p.f32(BIG1 + 12288, 4096)
    m_b = p.f32(RS, 4096)
    r_b = p.f32(RS + 4096, 4096)
    emit_stream_stats(p, xT_in, xn, m_b, r_b, sx2_b, xs0, xs1)
    emit_apply_ln(p, xT_in, xn, m_b, r_b, t_ap, xs0, xs1)

    wk = p.bf3(RS, DC, D)
    emit_load_bf(p, [(Wb["WinKR"][k], wk[:, k, :]) for k in range(DC)])
    k_all = p.bf3(BIG1, 32, D)
    mm0, ev0 = p.s_mm.n, p.s_ev.n

    def kb_tensor(tensor):
        it = 0
        for t in range(32):
            for h in range(2):
                ps = p.pp[it % 2]
                if it >= 2:
                    tensor.wait_ge(p.s_ev.h, ev0 + it - 1)
                for k in range(DC):
                    mm = tensor.matmul(ps[:, :], xn[:, k, bass.ts(t, 128)],
                                       wk[:, k, bass.ts(h, 512)],
                                       start=(k == 0), stop=(k == DC - 1))
                mm.then_inc(p.s_mm.h, 1)
                it += 1

    def kb_scalar(scalar):
        it = 0
        for t in range(32):
            for h in range(2):
                scalar.wait_ge(p.s_mm.h, mm0 + it + 1)
                scalar.activation(k_all[:, t, bass.ts(h, 512)], p.pp[it % 2][:, :],
                                  AF.Identity, scale=0.125).then_inc(p.s_ev.h, 1)
                it += 1

    run_block(nc, ("tensor", "scalar"), {"tensor": kb_tensor, "scalar": kb_scalar})
    p.s_mm.n = mm0 + 64
    p.s_ev.n = ev0 + 64
    p.barrier()
    ve0 = p.s_ve.n

    def k_clip(vector):
        for t in range(32):
            vector.tensor_scalar_max(k_all[:, t, :], k_all[:, t, :], -1.0)
            mm = vector.tensor_scalar_min(k_all[:, t, :], k_all[:, t, :], 1.0)
        mm.then_inc(p.s_ve.h, 1)

    run_block(nc, ("vector",), {"vector": k_clip})
    p.s_ve.n = ve0 + 1
    p.barrier()

    wv = p.bf3(RS, DC, D)
    emit_load_bf(p, [(Wb["WinVR"][k], wv[:, k, :]) for k in range(DC)])
    vt0 = p.bf(RS + 4096, 512)
    vt1 = p.bf(RS + 4352, 512)
    mm0, ev0 = p.s_mm.n, p.s_ev.n
    ad0 = [p.s_ad[0].n, p.s_ad[1].n]

    def vb_tensor(tensor):
        it = 0
        for t in range(32):
            for h in range(2):
                ps = p.pp[it % 2]
                if it >= 2:
                    tensor.wait_ge(p.s_ev.h, ev0 + it - 1)
                for k in range(DC):
                    mm = tensor.matmul(ps[:, :], xn[:, k, bass.ts(t, 128)],
                                       wv[:, k, bass.ts(h, 512)],
                                       start=(k == 0), stop=(k == DC - 1))
                mm.then_inc(p.s_mm.h, 1)
                it += 1

    def vb_scalar(scalar):
        it = 0
        for t in range(32):
            for h in range(2):
                scalar.wait_ge(p.s_mm.h, mm0 + it + 1)
                if it >= 2:
                    scalar.wait_ge(p.s_ad[it % 2].h,
                                   ad0[it % 2] + 16 * 4 * (it // 2))
                vt = (vt0, vt1)[it % 2]
                scalar.activation(vt[:, :], p.pp[it % 2][:, :], AF.Identity) \
                    .then_inc(p.s_ev.h, 1)
                scalar.wait_ge(p.s_ev.h, ev0 + it + 1)
                for pj in range(4):
                    pair = h * 4 + pj
                    scalar.dma_start(v_spill[pair][:, t, :],
                                     vt[:, bass.ts(pj, 128)]) \
                        .then_inc(p.s_ad[it % 2].h, 16)
                it += 1
        scalar.wait_ge(p.s_ad[0].h, ad0[0] + 16 * 4 * 32)
        scalar.wait_ge(p.s_ad[1].h, ad0[1] + 16 * 4 * 32)

    run_block(nc, ("tensor", "scalar"),
              {"tensor": vb_tensor, "scalar": vb_scalar})
    p.s_mm.n = mm0 + 64
    p.s_ev.n = ev0 + 64
    p.s_ad[0].n = ad0[0] + 16 * 4 * 32
    p.s_ad[1].n = ad0[1] + 16 * 4 * 32
    p.barrier()

    wqp = p.bf3(RS, DC, 128)
    qp = p.bf(RS + 512, 4096)
    tq = p.bf3(RS + 2560, NM, 1024)
    kf = p.bf(RS + 8192, 5632).rearrange("p (g m c) -> p g m c", m=NM, c=128)
    vp = p.bf3(RS + 11008, 32, 128)
    kvT2 = p.bf3(RS + 13056, NM, 64)
    kve = p.f32(RS + 13440, 704)
    kvb = p.bf(RS + 14144, 704)
    bbc = p.f32(RS + 14496, 704)
    otp = p.bf(RS + 15200, 4096)
    bRowA = p.bf(RS + 17248, 704)
    bRowB = p.bf(RS + 17600, 704)
    tmpq = p.bf(RS + 8192, 2048)

    for pair in range(NPAIR):
        emit_load_bf(p, [(Wb["WinQP"][pair],
                          wqp[:, :, :].rearrange("p a b -> p (a b)"))])
        sg = p.s_gd[0]
        g0 = sg.n

        def vp_gp(gpsimd, pair=pair):
            gpsimd.dma_start(vp[:, :, :], v_spill[pair][:, :, :]).then_inc(sg.h, 16)
            gpsimd.wait_ge(sg.h, g0 + 16)

        run_block(nc, ("gpsimd",), {"gpsimd": vp_gp})
        sg.n = g0 + 16
        p.barrier()

        ve0 = p.s_ve.n

        def beta_vec(vector, pair=pair):
            hA, hB = 2 * pair, 2 * pair + 1
            vector.tensor_copy(
                bRowA[0:1, :].rearrange("o (m c) -> o m c", c=64),
                beta_f[0:1, hA * 11:(hA + 1) * 11].unsqueeze(2)
                .broadcast_to((1, 11, 64)))
            vector.tensor_copy(
                bRowB[0:1, :].rearrange("o (m c) -> o m c", c=64),
                beta_f[0:1, hB * 11:(hB + 1) * 11].unsqueeze(2)
                .broadcast_to((1, 11, 64))).then_inc(p.s_ve.h, 1)

        run_block(nc, ("vector",), {"vector": beta_vec})
        p.s_ve.n = ve0 + 1
        p.barrier()

        mm0, ev0 = p.s_mm.n, p.s_ev.n

        def bb_tensor(tensor):
            for s0, s1 in ((0, 512), (512, 704)):
                tensor.matmul(p.ps_kv[0:64, s0:s1], p.ones128[0:1, 0:64],
                              bRowA[0:1, s0:s1], start=True, stop=True,
                              skip_group_check=True)
                mm = tensor.matmul(p.ps_kv[64:128, s0:s1], p.ones128[0:1, 0:64],
                                   bRowB[0:1, s0:s1], start=True, stop=True,
                                   tile_position=(0, 64), skip_group_check=True)
            mm.then_inc(p.s_mm.h, 1)

        def bb_scalar(scalar):
            scalar.wait_ge(p.s_mm.h, mm0 + 1)
            scalar.activation(bbc[:, :], p.ps_kv[:, :], AF.Identity) \
                .then_inc(p.s_ev.h, 1)

        run_block(nc, ("tensor", "scalar"), {"tensor": bb_tensor, "scalar": bb_scalar})
        p.s_mm.n = mm0 + 1
        p.s_ev.n = ev0 + 1
        p.barrier()

        emit_proj(p, n_o=1, n_k=DC,
                  lhsT=lambda o, k: wqp[:, k, :],
                  rhs=lambda k, s: xn[:, k, bass.ts(s, SW)],
                  evac_cb=lambda eng, o, s, ps: eng.activation(
                      qp[:, bass.ts(s, SW)], ps[:, :], AF.Identity, scale=0.125))
        ve0 = p.s_ve.n

        def q_clip(vector):
            vector.tensor_scalar_max(qp[:, :], qp[:, :], -1.0)
            vector.tensor_scalar_min(qp[:, :], qp[:, :], 1.0) \
                .then_inc(p.s_ve.h, 1)

        run_block(nc, ("vector",), {"vector": q_clip})
        p.s_ve.n = ve0 + 1
        p.barrier()

        mm0, ve0 = p.s_mm.n, p.s_ve.n

        def kv_vector(vector, pair=pair, ve0=ve0, mm0=mm0):
            for sg_i in range(8):
                if sg_i >= 1:
                    vector.wait_ge(p.s_mm.h, mm0 + 4 * sg_i)
                ksl = k_all[:, 4 * sg_i:4 * sg_i + 4,
                            bass.ts(pair, 128)]
                vector.memset(kf[:, :, 0, :], 1.0)
                vector.tensor_copy(kf[:, :, 1, :], ksl)
                for m in range(2, NM):
                    vector.tensor_mul(kf[:, :, m, :], ksl, kf[:, :, m - 1, :])
                    mm = vector.scalar_tensor_tensor(kf[:, :, m, :],
                                                     kf[:, :, m, :], 2.0,
                                                     kf[:, :, m - 2, :],
                                                     ALU.mult, ALU.subtract)
                mm.then_inc(p.s_ve.h, 1)

        def kv_tensor(tensor, ve0=ve0):
            for sg_i in range(8):
                tensor.wait_ge(p.s_ve.h, ve0 + sg_i + 1)
                for ti in range(4):
                    t = 4 * sg_i + ti
                    first = (t == 0)
                    last = (t == 31)
                    for m0, m1 in ((0, 8), (8, 11)):
                        s0, s1 = m0 * 64, m1 * 64
                        tensor.matmul(p.ps_kv[0:64, s0:s1], vp[:, t, 0:64],
                                      kf[:, ti, m0:m1, 0:64],
                                      start=first, stop=last,
                                      skip_group_check=True)
                        mm = tensor.matmul(p.ps_kv[64:128, s0:s1],
                                           vp[:, t, 64:128],
                                           kf[:, ti, m0:m1, 64:128],
                                           start=first, stop=last,
                                           tile_position=(0, 64),
                                           skip_group_check=True)
                    mm.then_inc(p.s_mm.h, 1)

        run_block(nc, ("vector", "tensor"),
                  {"vector": kv_vector, "tensor": kv_tensor})
        p.s_ve.n = ve0 + 8
        p.s_mm.n = mm0 + 32
        p.barrier()

        sc0, ve0 = p.s_sc.n, p.s_ve.n

        def kv_evac_sc(scalar):
            scalar.activation(kve[:, :], p.ps_kv[:, :], AF.Identity) \
                .then_inc(p.s_sc.h, 1)

        def kv_beta_vec(vector):
            vector.wait_ge(p.s_sc.h, sc0 + 1)
            vector.tensor_mul(kvb[:, :], kve[:, :], bbc[:, :]) \
                .then_inc(p.s_ve.h, 1)

        run_block(nc, ("scalar", "vector"),
                  {"scalar": kv_evac_sc, "vector": kv_beta_vec})
        p.s_sc.n = sc0 + 1
        p.s_ve.n = ve0 + 1
        p.barrier()

        kvb3 = kvb.rearrange("p (m c) -> p m c", c=64)
        mm0, ev0 = p.s_mm.n, p.s_ev.n

        def tr_tensor(tensor):
            n = 0
            for h in range(2):
                for m in range(NM):
                    if n >= 1:
                        tensor.wait_ge(p.s_ev.h, ev0 + n)
                    tensor.transpose(p.ps_tr[0:64, 0:64],
                                     kvb3[64 * h:64 * h + 64, m, :],
                                     idt[64 * h:64 * h + 64, 64 * h:64 * h + 64]) \
                        .then_inc(p.s_mm.h, 1)
                    n += 1


        def tr_scalar(scalar):
            n = 0
            for h in range(2):
                for m in range(NM):
                    scalar.wait_ge(p.s_mm.h, mm0 + n + 1)
                    scalar.activation(kvT2[64 * h:64 * h + 64, m, :],
                                      p.ps_tr[0:64, 0:64], AF.Identity) \
                        .then_inc(p.s_ev.h, 1)
                    n += 1

        run_block(nc, ("tensor", "scalar"),
                  {"tensor": tr_tensor, "scalar": tr_scalar})
        p.s_mm.n = mm0 + 2 * NM
        p.s_ev.n = ev0 + 2 * NM
        p.barrier()

        for qtr in range(4):
            ve0, mm0, ev0 = p.s_ve.n, p.s_mm.n, p.s_ev.n
            Q = qp.rearrange("p (q w) -> p q w", w=1024)[:, qtr, :]

            def tq_vec(vector, qtr=qtr, Q=Q):
                vector.memset(tq[:, 0, :], 1.0)
                vector.tensor_copy(tq[:, 1, :], Q)
                for m in range(2, NM):
                    vector.tensor_mul(tmpq[:, 0:1024], Q, tq[:, m - 1, :])
                    mm = vector.scalar_tensor_tensor(tq[:, m, :], tmpq[:, 0:1024],
                                                     2.0, tq[:, m - 2, :],
                                                     ALU.mult, ALU.subtract)
                mm.then_inc(p.s_ve.h, 1)

            def o_tensor(tensor, qtr=qtr, ve0=ve0, ev0=ev0):
                tensor.wait_ge(p.s_ve.h, ve0 + 1)
                for c in range(2):
                    ps = p.pp[c % 2]
                    if qtr * 2 + c >= 2:
                        tensor.wait_ge(p.s_ev.h, ev0 + c - 1)
                    for m in range(NM):
                        tensor.matmul(ps[0:64, :], kvT2[0:64, m, :],
                                      tq[0:64, m, bass.ts(c, SW)],
                                      start=(m == 0), stop=(m == NM - 1),
                                      skip_group_check=True)
                        mm = tensor.matmul(ps[64:128, :], kvT2[64:128, m, :],
                                           tq[64:128, m, bass.ts(c, SW)],
                                           start=(m == 0), stop=(m == NM - 1),
                                           tile_position=(64, 64),
                                           skip_group_check=True)
                    mm.then_inc(p.s_mm.h, 1)

            def o_scalar(scalar, qtr=qtr, mm0=mm0):
                for c in range(2):
                    scalar.wait_ge(p.s_mm.h, mm0 + c + 1)
                    scalar.activation(
                        otp[:, qtr * 1024 + c * SW:qtr * 1024 + (c + 1) * SW],
                        p.pp[c % 2][:, :], AF.Identity).then_inc(p.s_ev.h, 1)

            run_block(nc, ("vector", "tensor", "scalar"),
                      {"vector": tq_vec, "tensor": o_tensor, "scalar": o_scalar})
            p.s_ve.n = ve0 + 1
            p.s_mm.n = mm0 + 2
            p.s_ev.n = ev0 + 2
            p.barrier()

        sc0 = p.s_sc.n
        sa = p.s_ad[0]
        a0 = sa.n

        def sp_sc(scalar, pair=pair):
            scalar.dma_start(outT_spill[pair], otp[:, :]).then_inc(sa.h, 16)
            scalar.wait_ge(sa.h, a0 + 16)

        run_block(nc, ("scalar",), {"scalar": sp_sc})
        sa.n = a0 + 16
        p.barrier()

    zT = p.bf3(BIG1, DC, S)
    sg = p.s_gd[0]
    g0 = sg.n

    def rl_gp(gpsimd):
        for pr in range(NPAIR):
            gpsimd.dma_start(zT[:, pr, :], outT_spill[pr]).then_inc(sg.h, 16)
        gpsimd.wait_ge(sg.h, g0 + 16 * NPAIR)

    run_block(nc, ("gpsimd",), {"gpsimd": rl_gp})
    sg.n = g0 + 16 * NPAIR
    p.barrier()

    wo = p.bf3(RS, DC, D)
    emit_load_bf(p, [(Wb["WoutR"][k], wo[:, k, :]) for k in range(DC)])
    out0 = p.bf3(BIG0, DC, S)
    emit_proj(p, n_o=DC, n_k=DC,
              lhsT=lambda o, k: wo[:, k, bass.ts(o, 128)],
              rhs=lambda k, s: zT[:, k, bass.ts(s, SW)],
              evac_cb=lambda eng, o, s, ps: eng.activation(
                  out0[:, o, bass.ts(s, SW)], ps[:, :], AF.Identity))
    if dbg:
        emit_dump(p, dbg["f2"], out0)

    emit_post_ln_residual(p, out0=out0, osq_off=BIG1, xT_in=xT_in, xT_out=xT_out)


def host_block2(inputs):
    Wh = {}
    w_in = inputs["L1_attn_w_in"]
    Wh["b2_WinKR"] = np.ascontiguousarray(
        w_in[1024:2048, :].T.reshape(DC, 128, D))
    Wh["b2_WinVR"] = np.ascontiguousarray(
        w_in[2048:3072, :].T.reshape(DC, 128, D))
    wq = w_in[0:1024, :]
    t4 = wq.reshape(NCORE, 128, DC, 128)
    wqT = wq.T.reshape(DC, 128, NCORE, 128)
    Wh["b2_WinQP"] = np.ascontiguousarray(
        wqT.transpose(2, 1, 0, 3).reshape(NCORE, 128, D))
    Wh["b2_WoutR"] = rowsT(inputs["L1_attn_w_out"])
    Wh["b2_beta"] = np.ascontiguousarray(
        inputs["L1_attn_beta"].reshape(1, 176))
    Wh["b2_idt"] = np.eye(128, dtype=BF)
    return Wh


def declare_block2(nc):
    t = {}
    t["WinKR"] = nc.dram_tensor("b2_WinKR", [DC, 128, D], F32, kind="ExternalInput")
    t["WinVR"] = nc.dram_tensor("b2_WinVR", [DC, 128, D], F32, kind="ExternalInput")
    t["WinQP"] = nc.dram_tensor("b2_WinQP", [8, 128, D], F32, kind="ExternalInput")
    t["WoutR"] = nc.dram_tensor("b2_WoutR", [DC, 128, D], F32, kind="ExternalInput")
    t["beta"] = nc.dram_tensor("b2_beta", [1, 176], F32, kind="ExternalInput")
    t["idt"] = nc.dram_tensor("b2_idt", [128, 128], BF16, kind="ExternalInput")
    tb = {k: mirror_tensor(nc, t[k]) for k in
          ("WinKR", "WinVR", "WinQP", "WoutR")}
    W = {}
    W["WinKR"] = [t["WinKR"][k, :, :] for k in range(DC)]
    W["WinVR"] = [t["WinVR"][k, :, :] for k in range(DC)]
    W["WinQP"] = [t["WinQP"][pr, :, :] for pr in range(8)]
    W["WoutR"] = [t["WoutR"][k, :, :] for k in range(DC)]
    W["beta"] = t["beta"][:, :]
    W["idt"] = t["idt"][:, :]
    Wb = {}
    Wb["WinKR"] = [tb["WinKR"][k, :, :] for k in range(DC)]
    Wb["WinVR"] = [tb["WinVR"][k, :, :] for k in range(DC)]
    Wb["WinQP"] = [tb["WinQP"][pr, :, :] for pr in range(8)]
    Wb["WoutR"] = [tb["WoutR"][k, :, :] for k in range(DC)]
    mp = []
    for nm in ("WinKR", "WinVR", "WinQP", "WoutR"):
        for k in range(t[nm].shape[0]):
            mp.append((t[nm][k, :, :], tb[nm][k, :, :]))
    return W, Wb, mp


def build_block2_program(debug=False):
    nc = bass.Bass("TRN2", target_bir_lowering=False)
    xT_in_t = nc.dram_tensor("xT_in", [DC, 128, S], F32, kind="ExternalInput")
    xT_out_t = nc.dram_tensor("xT_out", [DC, 128, S], F32, kind="ExternalOutput")
    v_sp = nc.dram_tensor("v_sp", [8, 128, 32, 128], BF16, kind="Internal")
    o_sp = nc.dram_tensor("o_sp", [8, 128, S], BF16, kind="Internal")
    W, Wb, mp = declare_block2(nc)
    dbg = None
    if debug:
        t = nc.dram_tensor("dbg_f2", [DC, 128, S], BF16, kind="ExternalOutput")
        dbg = {"f2": [t[d, :, :] for d in range(DC)]}
    with ExitStack() as ctx:
        p = Prog(nc, ctx)
        emit_mirror_prelude(p, mp, RS)
        emit_block2(p, [xT_in_t[d, :, :] for d in range(DC)],
                    [xT_out_t[d, :, :] for d in range(DC)], W, Wb,
                    [v_sp[pr, :, :, :] for pr in range(8)],
                    [o_sp[pr, :, :] for pr in range(8)], dbg=dbg)
    return nc

def rowsT(w):
    K = w.shape[1]
    return np.ascontiguousarray(w.T.reshape(K // 128, 128, w.shape[0]))


def colsv(b):
    return np.ascontiguousarray(b.reshape(-1, 128, 1))


def host_block0(inputs):
    Wh = {}
    Wh["b0_w1T"] = rowsT(inputs["L0_attn_w1"])
    Wh["b0_w2T"] = rowsT(inputs["L0_attn_w2"])
    Wh["b0_w3T"] = rowsT(inputs["L0_attn_w3"])
    Wh["b0_dlT"] = rowsT(inputs["L0_attn_dl_w"])
    Wh["b0_AT"] = rowsT(inputs["L0_attn_A_w"])
    Wh["b0_BT"] = rowsT(inputs["L0_attn_B_w"])
    Wh["b0_CT"] = rowsT(inputs["L0_attn_C_w"])
    Wh["b0_DT"] = rowsT(inputs["L0_attn_D_w"])
    Wh["b0_convw"] = np.ascontiguousarray(
        inputs["L0_attn_conv_w"].reshape(8, 128, 3).transpose(1, 0, 2)
        .reshape(128, 24))
    Wh["b0_convb"] = np.ascontiguousarray(
        inputs["L0_attn_conv_b"].reshape(8, 128).T)
    Wh["b0_dlb"] = colsv(inputs["L0_attn_dl_b"])
    Wh["b0_Ab"] = colsv(inputs["L0_attn_A_b"])
    Wh["b0_Bb"] = colsv(inputs["L0_attn_B_b"])
    Wh["b0_CDb"] = np.ascontiguousarray(np.concatenate([
        inputs["L0_attn_C_b"].reshape(8, 128).T,
        inputs["L0_attn_D_b"].reshape(8, 128).T], axis=1))
    return Wh


def declare_block0(nc):
    t = {}
    t["w1T"] = nc.dram_tensor("b0_w1T", [DC, 128, D], F32, kind="ExternalInput")
    t["w2T"] = nc.dram_tensor("b0_w2T", [DC, 128, D], F32, kind="ExternalInput")
    t["w3T"] = nc.dram_tensor("b0_w3T", [DC, 128, D], F32, kind="ExternalInput")
    t["dlT"] = nc.dram_tensor("b0_dlT", [DC, 128, HID], F32, kind="ExternalInput")
    t["AT"] = nc.dram_tensor("b0_AT", [DC, 128, HID], F32, kind="ExternalInput")
    t["BT"] = nc.dram_tensor("b0_BT", [DC, 128, HID], F32, kind="ExternalInput")
    t["CT"] = nc.dram_tensor("b0_CT", [HC, 128, D], F32, kind="ExternalInput")
    t["DT"] = nc.dram_tensor("b0_DT", [DC, 128, D], F32, kind="ExternalInput")
    t["convw"] = nc.dram_tensor("b0_convw", [128, 24], F32, kind="ExternalInput")
    t["convb"] = nc.dram_tensor("b0_convb", [128, 8], F32, kind="ExternalInput")
    t["dlb"] = nc.dram_tensor("b0_dlb", [HC, 128, 1], F32, kind="ExternalInput")
    t["Ab"] = nc.dram_tensor("b0_Ab", [HC, 128, 1], F32, kind="ExternalInput")
    t["Bb"] = nc.dram_tensor("b0_Bb", [HC, 128, 1], F32, kind="ExternalInput")
    t["CDb"] = nc.dram_tensor("b0_CDb", [128, 16], F32, kind="ExternalInput")
    tb = {k: mirror_tensor(nc, t[k]) for k in
          ("w1T", "w2T", "w3T", "dlT", "AT", "BT", "CT", "DT")}

    W = {}
    W["w1T"] = [t["w1T"][k, :, :] for k in range(DC)]
    W["w2T"] = [t["w2T"][k, :, :] for k in range(DC)]
    W["w3T"] = [t["w3T"][k, :, :] for k in range(DC)]
    W["dlT"] = [[t["dlT"][k, :, bass.ts(c, 128)] for c in range(HC)]
                for k in range(DC)]
    W["AT"] = [[t["AT"][k, :, bass.ts(c, 128)] for c in range(HC)]
               for k in range(DC)]
    W["BT"] = [[t["BT"][k, :, bass.ts(c, 128)] for c in range(HC)]
               for k in range(DC)]
    W["CT"] = [t["CT"][k, :, :] for k in range(HC)]
    W["DT"] = [t["DT"][k, :, :] for k in range(DC)]
    W["conv_w"] = t["convw"][:, :]
    W["conv_b8"] = t["convb"][:, :]
    W["dlb"] = [t["dlb"][c, :, :] for c in range(HC)]
    W["Ab"] = [t["Ab"][c, :, :] for c in range(HC)]
    W["Bb"] = [t["Bb"][c, :, :] for c in range(HC)]
    W["CDb"] = t["CDb"][:, :]
    Wb = {}
    Wb["w1T"] = [tb["w1T"][k, :, :] for k in range(DC)]
    Wb["w2T"] = [tb["w2T"][k, :, :] for k in range(DC)]
    Wb["w3T"] = [tb["w3T"][k, :, :] for k in range(DC)]
    Wb["dlT"] = [[tb["dlT"][k, :, bass.ts(c, 128)] for c in range(HC)]
                 for k in range(DC)]
    Wb["AT"] = [[tb["AT"][k, :, bass.ts(c, 128)] for c in range(HC)]
                for k in range(DC)]
    Wb["BT"] = [[tb["BT"][k, :, bass.ts(c, 128)] for c in range(HC)]
                for k in range(DC)]
    Wb["CT"] = [tb["CT"][k, :, :] for k in range(HC)]
    Wb["DT"] = [tb["DT"][k, :, :] for k in range(DC)]
    mp = []
    for nm in ("w1T", "w2T", "w3T", "dlT", "AT", "BT", "CT", "DT"):
        for k in range(t[nm].shape[0]):
            mp.append((t[nm][k, :, :], tb[nm][k, :, :]))
    return W, Wb, mp


def build_block0_program(debug=False):
    nc = bass.Bass("TRN2", target_bir_lowering=False)
    xT_in_t = nc.dram_tensor("xT_in", [DC, 128, S], F32, kind="ExternalInput")
    xT_out_t = nc.dram_tensor("xT_out", [DC, 128, S], F32, kind="ExternalOutput")
    hkind = "ExternalOutput" if debug else "Internal"
    h_spill_t = nc.dram_tensor("h_spill", [HC, 128, S], BF16, kind=hkind)
    W, Wb, mp = declare_block0(nc)
    dbg = None
    if debug:
        dbg = {}
        for name in ("xn", "x0", "x1", "ssm", "out0"):
            t = nc.dram_tensor("dbg_" + name, [DC, 128, S], BF16,
                               kind="ExternalOutput")
            dbg[name] = [t[d, :, :] for d in range(DC)]
    with ExitStack() as ctx:
        p = Prog(nc, ctx)
        emit_mirror_prelude(p, mp, RS)
        emit_block0(p, [xT_in_t[d, :, :] for d in range(DC)],
                    [xT_out_t[d, :, :] for d in range(DC)], W, Wb,
                    [h_spill_t[c, :, :] for c in range(HC)], dbg=dbg)
    return nc


def build_fused_program():
    nc = bass.Bass("TRN2", target_bir_lowering=False)
    xT_in_t = nc.dram_tensor("xT_in", [DC, 128, S], F32, kind="ExternalInput")
    xT_out_t = nc.dram_tensor("xT_out", [DC, 128, S], F32, kind="ExternalOutput")
    x1T = nc.dram_tensor("x1T", [DC, 128, S], F32, kind="Internal")
    x2T = nc.dram_tensor("x2T", [DC, 128, S], F32, kind="Internal")
    x3T = nc.dram_tensor("x3T", [DC, 128, S], F32, kind="Internal")
    h_spill_t = nc.dram_tensor("h_spill", [HC, 128, S], BF16, kind="Internal")
    v_sp = nc.dram_tensor("v_sp", [8, 128, 32, 128], BF16, kind="Internal")
    o_sp = nc.dram_tensor("o_sp", [8, 128, S], BF16, kind="Internal")
    W0, Wb0, mp0 = declare_block0(nc)
    W1, Wb1, mp1 = declare_block1(nc)
    W2, Wb2, mp2 = declare_block2(nc)
    W3, Wb3, mp3 = declare_block3(nc)
    xin = [xT_in_t[d, :, :] for d in range(DC)]
    x1 = [x1T[d, :, :] for d in range(DC)]
    x2 = [x2T[d, :, :] for d in range(DC)]
    x3 = [x3T[d, :, :] for d in range(DC)]
    xout = [xT_out_t[d, :, :] for d in range(DC)]
    vs = [v_sp[pr, :, :, :] for pr in range(8)]
    os_ = [o_sp[pr, :, :] for pr in range(8)]
    with ExitStack() as ctx:
        p = Prog(nc, ctx)
        emit_mirror_prelude(p, mp0 + mp1 + mp2 + mp3, RS)
        with nc.named_scope("blk0"):
            emit_block0(p, xin, x1, W0, Wb0,
                        [h_spill_t[c, :, :] for c in range(HC)])
        with nc.named_scope("blk1"):
            emit_block1(p, x1, x2, W1, Wb1, vs, os_)
        with nc.named_scope("blk2"):
            emit_block2(p, x2, x3, W2, Wb2, vs, os_)
        with nc.named_scope("blk3"):
            emit_block3(p, x3, xout, W3, Wb3)
    return nc


def kernel(**inputs):
    x = np.asarray(inputs["L0_attn_x"], dtype=np.float32)
    Wh = {}
    Wh.update(host_block0(inputs))
    Wh.update(host_block1(inputs))
    Wh.update(host_block2(inputs))
    Wh.update(host_block3(inputs))
    nc = build_fused_program()
    in_maps = []
    for b in range(NCORE):
        m = dict(Wh)
        m["xT_in"] = np.ascontiguousarray(x[b].T).reshape(DC, 128, S)
        in_maps.append(m)
    res = run_bass_kernel_spmd(nc, in_maps, core_ids=list(range(NCORE)))
    out = np.empty((NCORE, S, D), dtype=np.float32)
    for b in range(NCORE):
        out[b] = res.results[b]["xT_out"].reshape(D, S).T
    return out


# version 04
# speedup vs baseline: 1.0262x; 1.0262x over previous
import numpy as np
import ml_dtypes
import concourse.bass as bass
import concourse.mybir as mybir
from concourse.bass_utils import run_bass_kernel_spmd
from contextlib import ExitStack

F32 = mybir.dt.float32
BF16 = mybir.dt.bfloat16
AF = mybir.ActivationFunctionType
ALU = mybir.AluOpType
BF = ml_dtypes.bfloat16

D = 1024
S = 4096
NCORE = 8
DC = D // 128
SW = 512
SC = S // SW
HID = 2048
HC = HID // 128
LNEPS = 1e-5

AWORDS = 50968
BIG0 = 0
BIG1 = 16384
RS = 32768


class Sem:
    __slots__ = ("h", "n")

    def __init__(self, nc, name):
        self.h = nc.alloc_semaphore(name)
        self.n = 0


class Prog:
    def __init__(self, nc, ctx):
        self.nc = nc
        self.ctx = ctx
        e = ctx.enter_context
        self.s_mm = Sem(nc, "s_mm")
        self.s_ev = Sem(nc, "s_ev")
        self.s_sc = Sem(nc, "s_sc")
        self.s_ve = Sem(nc, "s_ve")
        self.s_po = Sem(nc, "s_po")
        self.s_ld = [Sem(nc, "s_ld0"), Sem(nc, "s_ld1")]
        self.s_gd = [Sem(nc, "s_gd0"), Sem(nc, "s_gd1")]
        self.s_ad = [Sem(nc, "s_ad0"), Sem(nc, "s_ad1")]
        self.pp = [e(nc.psum_tensor(f"pp{i}", [128, SW], F32)) for i in range(4)]
        self.pq = [e(nc.psum_tensor(f"pq{i}", [128, SW], F32)) for i in range(2)]
        self.ps_tr = e(nc.psum_tensor("ps_tr", [128, 128], BF16))
        self.arena = e(nc.sbuf_tensor("arena", [128, AWORDS], F32))
        self.ones128 = e(nc.sbuf_tensor("ones128", [128, 128], BF16))
        self.zeros_w = e(nc.sbuf_tensor("zeros_w", [128, S], BF16))
        self.colb = e(nc.sbuf_tensor("colb", [128, 16], F32))
        self.small = e(nc.sbuf_tensor("small", [128, 64], F32))
        self.cst = e(nc.sbuf_tensor("cst", [128, 8], F32))
        self.zc = e(nc.sbuf_tensor("zc", [128, 32], F32))
        nc.const_aps.register(F32, LNEPS, self.cst[:, 0:1])
        nc.const_aps.register(F32, 1.0, self.cst[:, 1:2])
        nc.const_aps.register(F32, 1e-30, self.cst[:, 2:3])
        for s_ in (self.s_mm, self.s_ev, self.s_sc, self.s_ve, self.s_po,
                   *self.s_ld, *self.s_gd, *self.s_ad):
            nc.gpsimd.sem_clear(s_.h)
        nc.all_engine_barrier()

    def f32(self, off, n=4096):
        assert off + n <= AWORDS
        return self.arena[:, off:off + n]

    def f32_3(self, off, k, s):
        assert off + k * s <= AWORDS
        return self.arena[:, off:off + k * s].rearrange("p (k s) -> p k s", s=s)

    def bf(self, off, nbf):
        assert off + nbf // 2 <= AWORDS
        return self.arena[:, off:off + nbf // 2].bitcast(BF16)

    def bf3(self, off, k, s):
        assert off + k * s // 2 <= AWORDS
        return self.arena[:, off:off + k * s // 2].bitcast(BF16) \
            .rearrange("p (k s) -> p k s", s=s)

    def barrier(self):
        self.nc.all_engine_barrier()


def run_block(nc, order, bodies):
    with nc.Block() as blk:
        for name in order:
            if name in bodies:
                getattr(blk, name)(bodies[name])



def mirror_tensor(nc, t):
    return nc.dram_tensor(t.name + "_bf", list(t.shape), BF16, kind="Internal")


def emit_mirror_prelude(p, pairs3, stage_off):
    nc = p.nc
    stage = p.f32_3(stage_off, 2, 2048)
    stb = [p.bf(stage_off + 4096 + 1024 * i, 2048) for i in range(2)]
    ld0 = [p.s_ld[0].n, p.s_ld[1].n]
    gd0 = [p.s_gd[0].n, p.s_gd[1].n]
    ve0 = p.s_ve.n
    n = len(pairs3)

    def sync_body(sync):
        for i, (srcap, dstap) in enumerate(pairs3):
            w = srcap.shape[-1]
            pr = srcap.shape[0]
            if i >= 2:
                sync.wait_ge(p.s_ve.h, ve0 + i - 1)
            sync.dma_start(stage[0:pr, i % 2, 0:w], srcap) \
                .then_inc(p.s_ld[i % 2].h, 16)

    def vec_body(vector):
        for i, (srcap, dstap) in enumerate(pairs3):
            w = srcap.shape[-1]
            pr = srcap.shape[0]
            vector.wait_ge(p.s_ld[i % 2].h, ld0[i % 2] + 16 * (i // 2 + 1))
            if i >= 2:
                vector.wait_ge(p.s_gd[i % 2].h, gd0[i % 2] + 16 * (i // 2))
            vector.tensor_copy(stb[i % 2][0:pr, 0:w], stage[0:pr, i % 2, 0:w]) \
                .then_inc(p.s_ve.h, 1)

    def gp_body(gpsimd):
        for i, (srcap, dstap) in enumerate(pairs3):
            w = srcap.shape[-1]
            pr = srcap.shape[0]
            gpsimd.wait_ge(p.s_ve.h, ve0 + i + 1)
            gpsimd.dma_start(dstap, stb[i % 2][0:pr, 0:w]) \
                .then_inc(p.s_gd[i % 2].h, 16)
        gpsimd.wait_ge(p.s_gd[0].h, gd0[0] + 16 * ((n + 1) // 2))
        gpsimd.wait_ge(p.s_gd[1].h, gd0[1] + 16 * (n // 2))

    run_block(nc, ("sync", "vector", "gpsimd"),
              {"sync": sync_body, "vector": vec_body, "gpsimd": gp_body})
    p.s_ld[0].n = ld0[0] + 16 * ((n + 1) // 2)
    p.s_ld[1].n = ld0[1] + 16 * (n // 2)
    p.s_gd[0].n = gd0[0] + 16 * ((n + 1) // 2)
    p.s_gd[1].n = gd0[1] + 16 * (n // 2)
    p.s_ve.n = ve0 + n
    p.barrier()


def emit_load_bf(p, pairs):
    nc = p.nc
    sd = p.s_ld[0]
    ld0 = sd.n

    def sync_body(sync):
        for s_, d_ in pairs:
            sync.dma_start(d_, s_).then_inc(sd.h, 16)
        sync.wait_ge(sd.h, ld0 + 16 * len(pairs))

    run_block(nc, ("sync",), {"sync": sync_body})
    sd.n = ld0 + 16 * len(pairs)
    p.barrier()


def emit_load_cast(p, pairs, stage_off):
    direct = [(s_, d_) for (s_, d_) in pairs if d_.dtype == F32]
    casts = [(s_, d_) for (s_, d_) in pairs if d_.dtype != F32]
    nc = p.nc
    if direct:
        sd = p.s_ld[0]
        ld0 = sd.n

        def sync_d(sync):
            for s_, d_ in direct:
                sync.dma_start(d_, s_).then_inc(sd.h, 16)
            sync.wait_ge(sd.h, ld0 + 16 * len(direct))

        run_block(nc, ("sync",), {"sync": sync_d})
        sd.n = ld0 + 16 * len(direct)
    if casts:
        stage = p.f32_3(stage_off, 2, 2048)
        ld0 = [p.s_ld[0].n, p.s_ld[1].n]
        ve0 = p.s_ve.n

        def sync_body(sync):
            for i, (src, dst) in enumerate(casts):
                w = src.shape[-1]
                pr = src.shape[0] if len(src.shape) > 1 else 128
                if i >= 2:
                    sync.wait_ge(p.s_ve.h, ve0 + i - 1)
                sync.dma_start(stage[0:pr, i % 2, 0:w], src) \
                    .then_inc(p.s_ld[i % 2].h, 16)

        def vec_body(vector):
            for i, (src, dst) in enumerate(casts):
                w = src.shape[-1]
                pr = src.shape[0] if len(src.shape) > 1 else 128
                vector.wait_ge(p.s_ld[i % 2].h, ld0[i % 2] + 16 * (i // 2 + 1))
                vector.tensor_copy(dst, stage[0:pr, i % 2, 0:w]).then_inc(p.s_ve.h, 1)

        run_block(nc, ("sync", "vector"), {"sync": sync_body, "vector": vec_body})
        n_c = len(casts)
        p.s_ld[0].n = ld0[0] + 16 * ((n_c + 1) // 2)
        p.s_ld[1].n = ld0[1] + 16 * (n_c // 2)
        p.s_ve.n = ve0 + n_c
    p.barrier()


def emit_proj(p, *, n_o, n_k, lhsT, rhs, evac_cb, evac_engine="scalar", n_s=SC):
    nc = p.nc
    mm0, ev0 = p.s_mm.n, p.s_ev.n

    def tensor_body(tensor):
        for o in range(n_o):
            for s in range(n_s):
                it = o * n_s + s
                ps = p.pp[it % 2]
                if it >= 2:
                    tensor.wait_ge(p.s_ev.h, ev0 + it - 1)
                for k in range(n_k):
                    mm = tensor.matmul(ps[:, :], lhsT(o, k), rhs(k, s),
                                       start=(k == 0), stop=(k == n_k - 1))
                mm.then_inc(p.s_mm.h, 1)

    def evac_body(eng):
        for o in range(n_o):
            for s in range(n_s):
                it = o * n_s + s
                eng.wait_ge(p.s_mm.h, mm0 + it + 1)
                evac_cb(eng, o, s, p.pp[it % 2]).then_inc(p.s_ev.h, 1)

    run_block(nc, ("tensor", evac_engine),
              {"tensor": tensor_body, evac_engine: evac_body})
    p.s_mm.n = mm0 + n_o * n_s
    p.s_ev.n = ev0 + n_o * n_s
    p.barrier()


def emit_stat_pass(p, src3, dst_row, n_d=DC, inv_n=1.0 / D):
    nc = p.nc
    mm0, ev0 = p.s_mm.n, p.s_ev.n

    def tensor_body(tensor):
        for s in range(SC):
            ps = p.pp[s % 2]
            if s >= 2:
                tensor.wait_ge(p.s_ev.h, ev0 + s - 1)
            for d in range(n_d):
                mm = tensor.matmul(ps[:, :], p.ones128[:, :],
                                   src3[:, d, bass.ts(s, SW)],
                                   start=(d == 0), stop=(d == n_d - 1))
            mm.then_inc(p.s_mm.h, 1)

    def evac_body(scalar):
        for s in range(SC):
            scalar.wait_ge(p.s_mm.h, mm0 + s + 1)
            scalar.activation(dst_row[:, bass.ts(s, SW)], p.pp[s % 2][:, :],
                              AF.Identity, scale=inv_n).then_inc(p.s_ev.h, 1)

    run_block(nc, ("tensor", "scalar"), {"tensor": tensor_body, "scalar": evac_body})
    p.s_mm.n = mm0 + SC
    p.s_ev.n = ev0 + SC
    p.barrier()


def emit_sq_inplace(p, buf3, n_d=DC):
    sc0 = p.s_sc.n

    def sq(scalar):
        for d in range(n_d):
            scalar.activation(buf3[:, d, :], buf3[:, d, :], AF.Square)
        scalar.nop().then_inc(p.s_sc.h, 1)

    run_block(p.nc, ("scalar",), {"scalar": sq})
    p.s_sc.n = sc0 + 1
    p.barrier()


def emit_var_rstd(p, m_b, sx2_b, r_b):
    nc = p.nc
    ve0, sc0 = p.s_ve.n, p.s_sc.n

    def vec_var(vector):
        vector.tensor_tensor(r_b[:, :], m_b[:, :], m_b[:, :], ALU.mult)
        vector.tensor_sub(sx2_b[:, :], sx2_b[:, :], r_b[:, :]).then_inc(p.s_ve.h, 1)

    def sc_rstd(scalar):
        scalar.wait_ge(p.s_ve.h, ve0 + 1)
        scalar.activation(r_b[:, :], sx2_b[:, :], AF.Ln, bias=LNEPS)
        scalar.activation(r_b[:, :], r_b[:, :], AF.Exp, scale=-0.5) \
            .then_inc(p.s_sc.h, 1)

    run_block(nc, ("vector", "scalar"), {"vector": vec_var, "scalar": sc_rstd})
    p.s_ve.n = ve0 + 1
    p.s_sc.n = sc0 + 1
    p.barrier()


def emit_stream_stats(p, xT_in, xbf3, m_b, r_b, sx2_b, xs0, xs1):
    nc = p.nc
    ld0 = [p.s_ld[0].n, p.s_ld[1].n]
    ve0 = p.s_ve.n

    def ph1_sync(sync):
        for d in range(DC):
            if d >= 2:
                sync.wait_ge(p.s_ve.h, ve0 + d - 1)
            sync.dma_start((xs0, xs1)[d % 2], xT_in[d]) \
                .then_inc(p.s_ld[d % 2].h, 16)

    def ph1_vec(vector):
        for d in range(DC):
            vector.wait_ge(p.s_ld[d % 2].h, ld0[d % 2] + 16 * (d // 2 + 1))
            vector.tensor_copy(xbf3[:, d, :], (xs0, xs1)[d % 2]).then_inc(p.s_ve.h, 1)

    run_block(nc, ("sync", "vector"), {"sync": ph1_sync, "vector": ph1_vec})
    p.s_ld[0].n = ld0[0] + 16 * (DC // 2)
    p.s_ld[1].n = ld0[1] + 16 * (DC // 2)
    p.s_ve.n = ve0 + DC
    p.barrier()

    emit_stat_pass(p, xbf3, m_b)
    emit_sq_inplace(p, xbf3)
    emit_stat_pass(p, xbf3, sx2_b)
    emit_var_rstd(p, m_b, sx2_b, r_b)


def emit_apply_ln(p, xT_in, xn3, m_b, r_b, t_ap, xs0, xs1):
    nc = p.nc
    ld0 = [p.s_ld[0].n, p.s_ld[1].n]
    ve0 = p.s_ve.n

    def ap_sync(sync):
        for d in range(DC):
            if d >= 2:
                sync.wait_ge(p.s_ve.h, ve0 + 2 * (d - 1))
            sync.dma_start((xs0, xs1)[d % 2], xT_in[d]) \
                .then_inc(p.s_ld[d % 2].h, 16)

    def ap_vec(vector):
        for d in range(DC):
            vector.wait_ge(p.s_ld[d % 2].h, ld0[d % 2] + 16 * (d // 2 + 1))
            vector.tensor_sub(t_ap[:, :], (xs0, xs1)[d % 2], m_b[:, :])
            vector.tensor_mul(xn3[:, d, :], t_ap[:, :], r_b[:, :]) \
                .then_inc(p.s_ve.h, 2)

    run_block(nc, ("sync", "vector"), {"sync": ap_sync, "vector": ap_vec})
    p.s_ld[0].n = ld0[0] + 16 * (DC // 2)
    p.s_ld[1].n = ld0[1] + 16 * (DC // 2)
    p.s_ve.n = ve0 + 2 * DC
    p.barrier()


def emit_dump(p, dram_ap_list, buf3, n=DC):
    sg = p.s_gd[0]
    gd0 = sg.n

    def gp(gpsimd):
        for d in range(n):
            gpsimd.dma_start(dram_ap_list[d], buf3[:, d, :]).then_inc(sg.h, 16)
        gpsimd.wait_ge(sg.h, gd0 + 16 * n)

    run_block(p.nc, ("gpsimd",), {"gpsimd": gp})
    sg.n = gd0 + 16 * n
    p.barrier()


def emit_block0(p, xT_in, xT_out, W, Wb, h_spill, dbg=None):
    nc = p.nc

    sd0 = p.s_ld[0]
    ld0 = sd0.n

    def init_gp(gpsimd):
        gpsimd.memset(p.ones128[:, :], 1.0)
        gpsimd.memset(p.zeros_w[:, :], 0.0)
        gpsimd.memset(p.cst[:, 0:1], LNEPS)
        gpsimd.memset(p.cst[:, 1:2], 1.0)
        gpsimd.memset(p.cst[:, 2:3], 1e-30)

    def init_sync(sync):
        sync.dma_start(p.small[:, 0:24], W["conv_w"]).then_inc(sd0.h, 16)
        sync.dma_start(p.small[:, 24:32], W["conv_b8"]).then_inc(sd0.h, 16)
        sync.wait_ge(sd0.h, ld0 + 32)

    run_block(nc, ("gpsimd", "sync"), {"gpsimd": init_gp, "sync": init_sync})
    sd0.n = ld0 + 32
    p.barrier()

    xn = p.bf3(BIG0, DC, S)
    xs0 = p.f32(BIG1, 4096)
    xs1 = p.f32(BIG1 + 4096, 4096)
    sx2_b = p.f32(BIG1 + 8192, 4096)
    t_ap = p.f32(BIG1 + 12288, 4096)
    m_b = p.f32(RS, 4096)
    r_b = p.f32(RS + 4096, 4096)
    emit_stream_stats(p, xT_in, xn, m_b, r_b, sx2_b, xs0, xs1)
    emit_apply_ln(p, xT_in, xn, m_b, r_b, t_ap, xs0, xs1)
    if dbg:
        emit_dump(p, dbg["xn"], xn)

    w1b = p.bf3(RS + 8192, DC, D)
    emit_load_bf(p, [(Wb["w1T"][k], w1b[:, k, :]) for k in range(DC)])
    x0 = p.bf3(BIG1, DC, S)
    emit_proj(
        p, n_o=DC, n_k=DC,
        lhsT=lambda o, k: w1b[:, k, bass.ts(o, 128)],
        rhs=lambda k, s: xn[:, k, bass.ts(s, SW)],
        evac_cb=lambda eng, o, s, ps: eng.activation(
            x0[:, o, bass.ts(s, SW)], ps[:, :], AF.Identity),
    )
    if dbg:
        emit_dump(p, dbg["x0"], x0)

    x1 = p.bf3(BIG0, DC, S)
    t_a, t_b, t_c = p.f32(RS, 4096), p.f32(RS + 4096, 4096), p.f32(RS + 8192, 4096)
    cw = p.small
    ve0, sc0 = p.s_ve.n, p.s_sc.n

    def conv_vec(vector):
        for d in range(DC):
            if d >= 1:
                vector.wait_ge(p.s_sc.h, sc0 + d)
            x0c = x0[:, d, :]
            vector.tensor_scalar(t_a[:, :], x0c, cw[:, 3 * d + 1:3 * d + 2],
                                 cw[:, 24 + d:25 + d], ALU.mult, ALU.add)
            vector.scalar_tensor_tensor(t_b[:, 1:S], x0c[:, 0:S - 1],
                                        cw[:, 3 * d:3 * d + 1], t_a[:, 1:S],
                                        ALU.mult, ALU.add)
            vector.tensor_copy(t_b[:, 0:1], t_a[:, 0:1])
            vector.scalar_tensor_tensor(t_c[:, 0:S - 1], x0c[:, 1:S],
                                        cw[:, 3 * d + 2:3 * d + 3],
                                        t_b[:, 0:S - 1], ALU.mult, ALU.add)
            vector.tensor_copy(t_c[:, S - 1:S], t_b[:, S - 1:S]) \
                .then_inc(p.s_ve.h, 1)

    def conv_sc(scalar):
        for d in range(DC):
            scalar.wait_ge(p.s_ve.h, ve0 + d + 1)
            scalar.activation(x1[:, d, :], t_c[:, :], AF.Silu).then_inc(p.s_sc.h, 1)

    run_block(nc, ("vector", "scalar"), {"vector": conv_vec, "scalar": conv_sc})
    p.s_ve.n = ve0 + DC
    p.s_sc.n = sc0 + DC
    p.barrier()
    if dbg:
        emit_dump(p, dbg["x1"], x1)

    F1 = p.f32(BIG1, 4096)
    F2 = p.f32(BIG1 + 4096, 4096)
    F3 = p.f32(BIG1 + 8192, 4096)
    bf_a = p.bf(RS, 4096)
    bf_b = p.bf(RS + 2048, 4096)
    bf_c = p.bf(RS + 4096, 4096)
    wt = p.bf3(RS + 6144, 24, 128)
    for c in range(HC):
        pairs = []
        for k in range(DC):
            pairs.append((Wb["dlT"][k][c], wt[:, k, :]))
            pairs.append((Wb["AT"][k][c], wt[:, 8 + k, :]))
            pairs.append((Wb["BT"][k][c], wt[:, 16 + k, :]))
        pairs.append((W["dlb"][c], p.colb[:, 0:1]))
        pairs.append((W["Ab"][c], p.colb[:, 1:2]))
        pairs.append((W["Bb"][c], p.colb[:, 2:3]))
        emit_load_bf(p, pairs)

        emit_proj(
            p, n_o=3, n_k=DC,
            lhsT=lambda o, k: wt[:, 8 * o + k, :],
            rhs=lambda k, s: x1[:, k, bass.ts(s, SW)],
            evac_cb=lambda eng, o, s, ps: (
                eng.activation(F1[:, bass.ts(s, SW)], ps[:, :], AF.Exp,
                               bias=p.colb[:, 0:1]) if o == 0 else
                eng.activation(bf_a[:, bass.ts(s, SW)], ps[:, :], AF.Identity,
                               bias=p.colb[:, 1:2]) if o == 1 else
                eng.activation(bf_b[:, bass.ts(s, SW)], ps[:, :], AF.Identity,
                               bias=p.colb[:, 2:3])),
        )

        sc0, ve0, gd0 = p.s_sc.n, p.s_ve.n, p.s_gd[0].n

        def scan_sc(scalar):
            scalar.activation(F2[:, :], F1[:, :], AF.Ln, bias=1.0) \
                .then_inc(p.s_sc.h, 1)
            scalar.wait_ge(p.s_ve.h, ve0 + 1)
            scalar.activation(F3[:, :], F1[:, :], AF.Exp, scale=-1.0) \
                .then_inc(p.s_sc.h, 1)

        def scan_vec(vector):
            vector.wait_ge(p.s_sc.h, sc0 + 1)
            vector.tensor_tensor(F1[:, :], F2[:, :], bf_a[:, :], ALU.mult) \
                .then_inc(p.s_ve.h, 1)
            vector.wait_ge(p.s_sc.h, sc0 + 2)
            vector.tensor_tensor_scan(F2[:, :], F3[:, :], p.zeros_w[:, :],
                                      1.0, ALU.mult, ALU.add)
            vector.tensor_tensor(F1[:, :], bf_b[:, :], F2[:, :], ALU.mult)
            vector.tensor_tensor_scan(bf_c[:, :], F1[:, :], p.zeros_w[:, :],
                                      0.0, ALU.add, ALU.add).then_inc(p.s_ve.h, 1)

        def scan_gp(gpsimd, c=c):
            gpsimd.wait_ge(p.s_ve.h, ve0 + 2)
            gpsimd.dma_start(h_spill[c], bf_c[:, :]).then_inc(p.s_gd[0].h, 16)
            gpsimd.wait_ge(p.s_gd[0].h, gd0 + 16)

        run_block(nc, ("scalar", "vector", "gpsimd"),
                  {"scalar": scan_sc, "vector": scan_vec, "gpsimd": scan_gp})
        p.s_sc.n = sc0 + 2
        p.s_ve.n = ve0 + 2
        p.s_gd[0].n = gd0 + 16
        p.barrier()

    Cb_ = p.bf3(RS, HC, D)
    Db_ = p.bf3(RS + 8192, DC, D)
    pairs = [(Wb["CT"][k], Cb_[:, k, :]) for k in range(HC)]
    pairs += [(Wb["DT"][k], Db_[:, k, :]) for k in range(DC)]
    pairs.append((W["CDb"], p.colb[:, 0:16]))
    emit_load_bf(p, pairs)
    ve0 = p.s_ve.n

    def cb_vec(vector):
        vector.tensor_add(p.colb[:, 0:8], p.colb[:, 0:8], p.colb[:, 8:16]) \
            .then_inc(p.s_ve.h, 1)

    run_block(nc, ("vector",), {"vector": cb_vec})
    p.s_ve.n = ve0 + 1
    p.barrier()

    hsl = p.bf3(RS + 12288, HC, SW)
    ssm = p.bf3(BIG1, DC, S)
    mm0, ev0 = p.s_mm.n, p.s_ev.n
    gd0 = [p.s_gd[0].n, p.s_gd[1].n]

    def ph5_gp(gpsimd):
        for s in range(SC):
            if s >= 1:
                gpsimd.wait_ge(p.s_mm.h, mm0 + s * DC)
            for c2 in range(HC):
                gpsimd.dma_start(
                    hsl[:, c2, :],
                    h_spill[c2].rearrange("p (sc w) -> p sc w", w=SW)[:, s, :]
                ).then_inc(p.s_gd[s % 2].h, 16)

    def ph5_tensor(tensor):
        it = 0
        for s in range(SC):
            tensor.wait_ge(p.s_gd[s % 2].h,
                           gd0[s % 2] + 16 * HC * (s // 2 + 1))
            for d in range(DC):
                ps = p.pp[it % 2]
                if it >= 2:
                    tensor.wait_ge(p.s_ev.h, ev0 + it - 1)
                for c2 in range(HC):
                    tensor.matmul(ps[:, :], Cb_[:, c2, bass.ts(d, 128)],
                                  hsl[:, c2, :], start=(c2 == 0), stop=False)
                for d1 in range(DC):
                    mm = tensor.matmul(ps[:, :], Db_[:, d1, bass.ts(d, 128)],
                                       x1[:, d1, bass.ts(s, SW)],
                                       start=False, stop=(d1 == DC - 1))
                mm.then_inc(p.s_mm.h, 1)
                it += 1

    def ph5_scalar(scalar):
        it = 0
        for s in range(SC):
            for d in range(DC):
                scalar.wait_ge(p.s_mm.h, mm0 + it + 1)
                scalar.activation(ssm[:, d, bass.ts(s, SW)], p.pp[it % 2][:, :],
                                  AF.Identity, bias=p.colb[:, d:d + 1]) \
                    .then_inc(p.s_ev.h, 1)
                it += 1

    run_block(nc, ("gpsimd", "tensor", "scalar"),
              {"gpsimd": ph5_gp, "tensor": ph5_tensor, "scalar": ph5_scalar})
    p.s_gd[0].n = gd0[0] + 16 * HC * (SC // 2)
    p.s_gd[1].n = gd0[1] + 16 * HC * (SC // 2)
    p.s_mm.n = mm0 + SC * DC
    p.s_ev.n = ev0 + SC * DC
    p.barrier()
    if dbg:
        emit_dump(p, dbg["ssm"], ssm)

    w2b = p.bf3(RS, DC, D)
    emit_load_bf(p, [(Wb["w2T"][k], w2b[:, k, :]) for k in range(DC)])
    xres = p.bf3(BIG0, DC, S)
    emit_proj(
        p, n_o=DC, n_k=DC,
        lhsT=lambda o, k: w2b[:, k, bass.ts(o, 128)],
        rhs=lambda k, s: ssm[:, k, bass.ts(s, SW)],
        evac_cb=lambda eng, o, s, ps: eng.activation(
            xres[:, o, bass.ts(s, SW)], ps[:, :], AF.Silu),
    )
    ve0 = p.s_ve.n

    def z_vec(vector):
        for d in range(DC):
            vector.tensor_tensor(xres[:, d, :], ssm[:, d, :], xres[:, d, :],
                                 ALU.mult)
        vector.nop().then_inc(p.s_ve.h, 1)

    run_block(nc, ("vector",), {"vector": z_vec})
    p.s_ve.n = ve0 + 1
    p.barrier()

    w3b = p.bf3(RS, DC, D)
    emit_load_bf(p, [(Wb["w3T"][k], w3b[:, k, :]) for k in range(DC)])
    out0 = ssm
    emit_proj(
        p, n_o=DC, n_k=DC,
        lhsT=lambda o, k: w3b[:, k, bass.ts(o, 128)],
        rhs=lambda k, s: xres[:, k, bass.ts(s, SW)],
        evac_cb=lambda eng, o, s, ps: eng.tensor_add(
            out0[:, o, bass.ts(s, SW)], ps[:, :], ssm[:, o, bass.ts(s, SW)]),
        evac_engine="vector",
    )
    if dbg:
        emit_dump(p, dbg["out0"], out0)

    emit_post_ln_residual(p, out0=out0, osq_off=BIG0, xT_in=xT_in, xT_out=xT_out)


def emit_post_ln_residual(p, *, out0, osq_off, xT_in, xT_out):
    nc = p.nc
    osq = p.bf3(osq_off, DC, S)
    sc0 = p.s_sc.n

    def sq_sc(scalar):
        for d in range(DC):
            scalar.activation(osq[:, d, :], out0[:, d, :], AF.Square)
        scalar.nop().then_inc(p.s_sc.h, 1)

    run_block(nc, ("scalar",), {"scalar": sq_sc})
    p.s_sc.n = sc0 + 1
    p.barrier()

    m_b = p.f32(RS, 4096)
    r_b = p.f32(RS + 4096, 4096)
    sx2_b = p.f32(RS + 8192, 4096)
    emit_stat_pass(p, out0, m_b)
    emit_stat_pass(p, osq, sx2_b)
    emit_var_rstd(p, m_b, sx2_b, r_b)

    xs0 = p.f32(osq_off, 4096)
    xs1 = p.f32(osq_off + 4096, 4096)
    ot0 = p.f32(osq_off + 8192, 4096)
    ot1 = p.f32(osq_off + 12288, 4096)
    ld0 = [p.s_ld[0].n, p.s_ld[1].n]
    gd0 = [p.s_gd[0].n, p.s_gd[1].n]
    ve0 = p.s_ve.n

    def res_sync(sync):
        for d in range(DC):
            if d >= 2:
                sync.wait_ge(p.s_ve.h, ve0 + 3 * (d - 1))
            sync.dma_start((xs0, xs1)[d % 2], xT_in[d]) \
                .then_inc(p.s_ld[d % 2].h, 16)

    def res_vec(vector):
        for d in range(DC):
            vector.wait_ge(p.s_ld[d % 2].h, ld0[d % 2] + 16 * (d // 2 + 1))
            if d >= 2:
                vector.wait_ge(p.s_gd[d % 2].h, gd0[d % 2] + 16 * (d // 2))
            tt = (ot0, ot1)[d % 2]
            vector.tensor_sub(tt[:, :], out0[:, d, :], m_b[:, :])
            vector.tensor_mul(tt[:, :], tt[:, :], r_b[:, :])
            vector.tensor_add(tt[:, :], (xs0, xs1)[d % 2], tt[:, :]) \
                .then_inc(p.s_ve.h, 3)

    def res_gp(gpsimd):
        for d in range(DC):
            gpsimd.wait_ge(p.s_ve.h, ve0 + 3 * (d + 1))
            gpsimd.dma_start(xT_out[d], (ot0, ot1)[d % 2][:, :]) \
                .then_inc(p.s_gd[d % 2].h, 16)
        gpsimd.wait_ge(p.s_gd[0].h, gd0[0] + 16 * (DC // 2))
        gpsimd.wait_ge(p.s_gd[1].h, gd0[1] + 16 * (DC // 2))

    run_block(nc, ("sync", "vector", "gpsimd"),
              {"sync": res_sync, "vector": res_vec, "gpsimd": res_gp})
    p.s_ld[0].n = ld0[0] + 16 * (DC // 2)
    p.s_ld[1].n = ld0[1] + 16 * (DC // 2)
    p.s_ve.n = ve0 + 3 * DC
    p.s_gd[0].n = gd0[0] + 16 * (DC // 2)
    p.s_gd[1].n = gd0[1] + 16 * (DC // 2)
    p.barrier()



def emit_block3(p, xT_in, xT_out, W, Wb, dbg=None):
    nc = p.nc

    sd0 = p.s_ld[0]
    ld0 = sd0.n

    def init_gp(gpsimd):
        gpsimd.memset(p.ones128[:, :], 1.0)
        gpsimd.memset(p.cst[:, 0:1], LNEPS)
        gpsimd.memset(p.cst[:, 1:2], 1.0)
        gpsimd.memset(p.cst[:, 2:3], 1e-30)

    def init_sync(sync):
        sync.dma_start(p.small[0:64, 0:50], W["inner"]).then_inc(sd0.h, 16)
        sync.wait_ge(sd0.h, ld0 + 16)

    run_block(nc, ("gpsimd", "sync"), {"gpsimd": init_gp, "sync": init_sync})
    sd0.n = ld0 + 16
    p.barrier()

    xn = p.bf3(BIG0, DC, S)
    xs0 = p.f32(BIG1, 4096)
    xs1 = p.f32(BIG1 + 4096, 4096)
    sx2_b = p.f32(BIG1 + 8192, 4096)
    t_ap = p.f32(BIG1 + 12288, 4096)
    m_b = p.f32(RS, 4096)
    r_b = p.f32(RS + 4096, 4096)
    emit_stream_stats(p, xT_in, xn, m_b, r_b, sx2_b, xs0, xs1)
    emit_apply_ln(p, xT_in, xn, m_b, r_b, t_ap, xs0, xs1)

    winb = p.bf3(RS + 12288, DC, 64)
    emit_load_bf(p, [(Wb["WinT"][k], winb[:, k, :]) for k in range(DC)])
    sT = p.f32(RS + 8192, 4096)
    mm0, ev0 = p.s_mm.n, p.s_ev.n

    def win_tensor(tensor):
        for s in range(SC):
            ps = p.pp[s % 2]
            if s >= 2:
                tensor.wait_ge(p.s_ev.h, ev0 + s - 1)
            for k in range(DC):
                mm = tensor.matmul(ps[0:64, :], winb[:, k, :],
                                   xn[:, k, bass.ts(s, SW)],
                                   start=(k == 0), stop=(k == DC - 1))
            mm.then_inc(p.s_mm.h, 1)

    def win_evac(scalar):
        for s in range(SC):
            scalar.wait_ge(p.s_mm.h, mm0 + s + 1)
            scalar.activation(sT[0:64, bass.ts(s, SW)], p.pp[s % 2][0:64, :],
                              AF.Identity, bias=p.small[0:64, 48:49]) \
                .then_inc(p.s_ev.h, 1)

    run_block(nc, ("tensor", "scalar"), {"tensor": win_tensor, "scalar": win_evac})
    p.s_mm.n = mm0 + SC
    p.s_ev.n = ev0 + SC
    p.barrier()

    g = p.f32(BIG1, 4096)
    tj = p.f32(BIG1 + 4096, 4096)
    sw = p.small
    ve0 = p.s_ve.n

    def inner_vec(vector):
        for j in range(16):
            vector.tensor_scalar(tj[0:64, :], sT[0:64, :], sw[0:64, j:j + 1],
                                 sw[0:64, 16 + j:17 + j], ALU.mult, ALU.add)
            vector.tensor_scalar_max(tj[0:64, :], tj[0:64, :], 0.0)
            if j == 0:
                vector.tensor_scalar_mul(g[0:64, :], tj[0:64, :],
                                         sw[0:64, 32 + j:33 + j])
            else:
                vector.scalar_tensor_tensor(g[0:64, :], tj[0:64, :],
                                            sw[0:64, 32 + j:33 + j], g[0:64, :],
                                            ALU.mult, ALU.add)
        vector.tensor_scalar_add(g[0:64, :], g[0:64, :], sw[0:64, 49:50])
        vector.tensor_copy(gbf[0:64, :], g[0:64, :]).then_inc(p.s_ve.h, 1)

    gbf = p.bf(BIG1 + 8192, 4096)
    run_block(nc, ("vector",), {"vector": inner_vec})
    p.s_ve.n = ve0 + 1
    p.barrier()

    woutb = p.bf(RS + 12288, 2048)
    woutb = p.bf3(RS + 12288, 1, 1024)
    emit_load_bf(p, [(Wb["Wout64"], woutb[0:64, 0, :])])
    boutc = p.colb[:, 0:8]
    sdl = p.s_ld[0]
    ldb = sdl.n

    def bout_sync(sync):
        sync.dma_start(boutc, W["bout8"]).then_inc(sdl.h, 16)
        sync.wait_ge(sdl.h, ldb + 16)

    run_block(nc, ("sync",), {"sync": bout_sync})
    sdl.n = ldb + 16
    p.barrier()

    out0 = p.bf3(BIG0, DC, S)
    mm0, ev0 = p.s_mm.n, p.s_ev.n

    def wout_tensor(tensor):
        it = 0
        for o in range(DC):
            for s in range(SC):
                ps = p.pp[it % 2]
                if it >= 2:
                    tensor.wait_ge(p.s_ev.h, ev0 + it - 1)
                mm = tensor.matmul(ps[:, :], woutb[0:64, 0, bass.ts(o, 128)],
                                   gbf[0:64, bass.ts(s, SW)],
                                   start=True, stop=True)
                mm.then_inc(p.s_mm.h, 1)
                it += 1

    def wout_evac(scalar):
        it = 0
        for o in range(DC):
            for s in range(SC):
                scalar.wait_ge(p.s_mm.h, mm0 + it + 1)
                scalar.activation(out0[:, o, bass.ts(s, SW)], p.pp[it % 2][:, :],
                                  AF.Identity, bias=p.colb[:, o:o + 1]) \
                    .then_inc(p.s_ev.h, 1)
                it += 1

    run_block(nc, ("tensor", "scalar"), {"tensor": wout_tensor, "scalar": wout_evac})
    p.s_mm.n = mm0 + DC * SC
    p.s_ev.n = ev0 + DC * SC
    p.barrier()
    if dbg:
        emit_dump(p, dbg["f3"], out0)

    emit_post_ln_residual(p, out0=out0, osq_off=BIG1, xT_in=xT_in, xT_out=xT_out)


def host_block3(inputs):
    Wh = {}
    Wh["b3_WinT"] = np.ascontiguousarray(
        inputs["L1_ffn_Win"].reshape(DC, 128, 64))
    inner = np.concatenate([
        inputs["L1_ffn_W1"], inputs["L1_ffn_b1"], inputs["L1_ffn_W2"],
        inputs["L1_ffn_bin_"].reshape(64, 1), inputs["L1_ffn_b2"].reshape(64, 1),
    ], axis=1)
    Wh["b3_inner"] = np.ascontiguousarray(inner)
    Wh["b3_Wout"] = np.ascontiguousarray(inputs["L1_ffn_Wout"])
    Wh["b3_bout"] = np.ascontiguousarray(
        inputs["L1_ffn_bout"].reshape(DC, 128).T)
    return Wh


def declare_block3(nc):
    t = {}
    t["WinT"] = nc.dram_tensor("b3_WinT", [DC, 128, 64], F32, kind="ExternalInput")
    t["inner"] = nc.dram_tensor("b3_inner", [64, 50], F32, kind="ExternalInput")
    t["Wout"] = nc.dram_tensor("b3_Wout", [64, 1024], F32, kind="ExternalInput")
    t["bout"] = nc.dram_tensor("b3_bout", [128, 8], F32, kind="ExternalInput")
    tb = {"WinT": mirror_tensor(nc, t["WinT"]),
          "Wout": mirror_tensor(nc, t["Wout"])}
    W = {}
    W["WinT"] = [t["WinT"][k, :, :] for k in range(DC)]
    W["inner"] = t["inner"][:, :]
    W["Wout64"] = t["Wout"][:, :]
    W["bout8"] = t["bout"][:, :]
    Wb = {}
    Wb["WinT"] = [tb["WinT"][k, :, :] for k in range(DC)]
    Wb["Wout64"] = tb["Wout"][:, :]
    mp = [(t["WinT"][k, :, :], tb["WinT"][k, :, :]) for k in range(DC)]
    mp.append((t["Wout"][:, :], tb["Wout"][:, :]))
    return W, Wb, mp


def build_block3_program(debug=False):
    nc = bass.Bass("TRN2", target_bir_lowering=False)
    xT_in_t = nc.dram_tensor("xT_in", [DC, 128, S], F32, kind="ExternalInput")
    xT_out_t = nc.dram_tensor("xT_out", [DC, 128, S], F32, kind="ExternalOutput")
    W, Wb, mp = declare_block3(nc)
    dbg = None
    if debug:
        t = nc.dram_tensor("dbg_f3", [DC, 128, S], BF16, kind="ExternalOutput")
        dbg = {"f3": [t[d, :, :] for d in range(DC)]}
    with ExitStack() as ctx:
        p = Prog(nc, ctx)
        emit_mirror_prelude(p, mp, RS)
        emit_block3(p, [xT_in_t[d, :, :] for d in range(DC)],
                    [xT_out_t[d, :, :] for d in range(DC)], W, Wb, dbg=dbg)
    return nc



def emit_dump_flat(p, dram_ap, buf):
    sg = p.s_gd[0]
    g0 = sg.n

    def gp(gpsimd):
        gpsimd.dma_start(dram_ap, buf).then_inc(sg.h, 16)
        gpsimd.wait_ge(sg.h, g0 + 16)

    run_block(p.nc, ("gpsimd",), {"gpsimd": gp})
    sg.n = g0 + 16
    p.barrier()


def emit_block1(p, xT_in, xT_out, W, Wb, v_spill, outT_spill, dbg=None):
    nc = p.nc
    NPAIR = 8
    NKT = 32
    GRP = 4

    def init_gp(gpsimd):
        gpsimd.memset(p.ones128[:, :], 1.0)
        gpsimd.memset(p.cst[:, 0:1], LNEPS)
        gpsimd.memset(p.cst[:, 1:2], 1.0)
        gpsimd.memset(p.cst[:, 2:3], 1e-30)

    run_block(nc, ("gpsimd",), {"gpsimd": init_gp})
    p.barrier()

    xn = p.bf3(BIG0, DC, S)
    xs0 = p.f32(BIG1, 4096)
    xs1 = p.f32(BIG1 + 4096, 4096)
    sx2_b = p.f32(BIG1 + 8192, 4096)
    t_ap = p.f32(BIG1 + 12288, 4096)
    m_b = p.f32(RS, 4096)
    r_b = p.f32(RS + 4096, 4096)
    with nc.named_scope("b1_ln"):
        emit_stream_stats(p, xT_in, xn, m_b, r_b, sx2_b, xs0, xs1)
        emit_apply_ln(p, xT_in, xn, m_b, r_b, t_ap, xs0, xs1)

    wv = p.bf3(RS, DC, D)
    emit_load_bf(p, [(Wb["WvR"][k], wv[:, k, :]) for k in range(DC)])
    vt0 = p.bf(RS + 4096, 512)
    vt1 = p.bf(RS + 4352, 512)
    mm0, ev0 = p.s_mm.n, p.s_ev.n
    ad0 = [p.s_ad[0].n, p.s_ad[1].n]

    def vb_tensor(tensor):
        it = 0
        for t in range(32):
            for h in range(2):
                ps = p.pp[it % 2]
                if it >= 2:
                    tensor.wait_ge(p.s_ev.h, ev0 + it - 1)
                for k in range(DC):
                    mm = tensor.matmul(ps[:, :], xn[:, k, bass.ts(t, 128)],
                                       wv[:, k, bass.ts(h, 512)],
                                       start=(k == 0), stop=(k == DC - 1))
                mm.then_inc(p.s_mm.h, 1)
                it += 1

    def vb_scalar(scalar):
        it = 0
        for t in range(32):
            for h in range(2):
                scalar.wait_ge(p.s_mm.h, mm0 + it + 1)
                if it >= 2:
                    scalar.wait_ge(p.s_ad[it % 2].h,
                                   ad0[it % 2] + 16 * 4 * (it // 2))
                vt = (vt0, vt1)[it % 2]
                scalar.activation(vt[:, :], p.pp[it % 2][:, :], AF.Identity) \
                    .then_inc(p.s_ev.h, 1)
                scalar.wait_ge(p.s_ev.h, ev0 + it + 1)
                for pj in range(4):
                    pair = h * 4 + pj
                    scalar.dma_start(v_spill[pair][:, t, :],
                                     vt[:, bass.ts(pj, 128)]) \
                        .then_inc(p.s_ad[it % 2].h, 16)
                it += 1
        scalar.wait_ge(p.s_ad[0].h, ad0[0] + 16 * 4 * 32)
        scalar.wait_ge(p.s_ad[1].h, ad0[1] + 16 * 4 * 32)

    run_block(nc, ("tensor", "scalar"),
              {"tensor": vb_tensor, "scalar": vb_scalar})
    p.s_mm.n = mm0 + 64
    p.s_ev.n = ev0 + 64
    p.s_ad[0].n = ad0[0] + 16 * 4 * 32
    p.s_ad[1].n = ad0[1] + 16 * 4 * 32
    p.barrier()

    wqk = p.bf3(RS, 16, 128)
    qT = p.bf(RS + 1024, 4096)
    kT = p.bf(RS + 3072, 4096)
    vp = p.bf3(RS + 5120, 32, 128)
    acc = p.f32(RS + 7168, 4096)
    sq = p.bf(RS + 11264, 4096)
    vpr = p.bf3(RS + 13312, GRP, 128)
    E = p.bf3(BIG1, 2 * GRP, S)
    nrm = p.small
    zb1 = p.bf(RS + 13568, 2048)
    zb2 = p.f32(RS + 14592, 1024)

    for pair in range(NPAIR):
        emit_load_bf(p, [(Wb["WqP"][pair],
                          wqk[:, 0:8, :].rearrange("p a b -> p (a b)")),
                         (Wb["WkP"][pair],
                          wqk[:, 8:16, :].rearrange("p a b -> p (a b)"))])
        sg = p.s_gd[0]
        g0 = sg.n

        def vp_gp(gpsimd, pair=pair):
            gpsimd.dma_start(vp[:, :, :], v_spill[pair][:, :, :]).then_inc(sg.h, 16)
            gpsimd.wait_ge(sg.h, g0 + 16)

        run_block(nc, ("gpsimd",), {"gpsimd": vp_gp})
        sg.n = g0 + 16
        p.barrier()

        sid = nc.enter_named_scope("b1_c0123", False)[0]
        sidq = nc.enter_named_scope("b1_c12", False)[0]
        emit_proj(p, n_o=1, n_k=DC,
                  lhsT=lambda o, k: wqk[:, k, :],
                  rhs=lambda k, s: xn[:, k, bass.ts(s, SW)],
                  evac_cb=lambda eng, o, s, ps: eng.activation(
                      qT[:, bass.ts(s, SW)], ps[:, :], AF.Identity,
                      scale=1.0 / 32.0))
        emit_proj(p, n_o=1, n_k=DC,
                  lhsT=lambda o, k: wqk[:, 8 + k, :],
                  rhs=lambda k, s: xn[:, k, bass.ts(s, SW)],
                  evac_cb=lambda eng, o, s, ps: eng.activation(
                      kT[:, bass.ts(s, SW)], ps[:, :], AF.Identity))

        nc.leave_named_scope("b1_c12", sidq, False)
        sidn = nc.enter_named_scope("b1_c3", False)[0]
        sc0 = p.s_sc.n

        def sq_q(scalar):
            scalar.activation(sq[:, :], qT[:, :], AF.Square).then_inc(p.s_sc.h, 1)

        run_block(nc, ("scalar",), {"scalar": sq_q})
        p.s_sc.n = sc0 + 1
        p.barrier()
        emit_norm_max(p, sq, nrm, 0)
        sc0 = p.s_sc.n

        def sq_k(scalar):
            scalar.activation(sq[:, :], kT[:, :], AF.Square).then_inc(p.s_sc.h, 1)

        run_block(nc, ("scalar",), {"scalar": sq_k})
        p.s_sc.n = sc0 + 1
        p.barrier()
        emit_norm_max(p, sq, nrm, 2)

        ve0, sc0 = p.s_ve.n, p.s_sc.n

        def mM_vec(vector):
            vector.tensor_tensor(nrm[:, 36:38], nrm[:, 32:34], nrm[:, 34:36],
                                 ALU.mult).then_inc(p.s_ve.h, 1)

        def mM_sc(scalar):
            scalar.wait_ge(p.s_ve.h, ve0 + 1)
            scalar.activation(nrm[:, 38:40], nrm[:, 36:38], AF.Ln, bias=1e-30)
            scalar.activation(nrm[:, 38:40], nrm[:, 38:40], AF.Exp, scale=0.5)
            scalar.mul(p.colb[:, 8:10], nrm[:, 38:40], -1.0).then_inc(p.s_sc.h, 1)

        run_block(nc, ("vector", "scalar"), {"vector": mM_vec, "scalar": mM_sc})
        p.s_ve.n = ve0 + 1
        p.s_sc.n = sc0 + 1
        p.barrier()

        if dbg and pair == 0:
            emit_dump_flat(p, dbg["vp"],
                           vp[:, :, :].rearrange("p a b -> p (a b)"))
            emit_dump_flat(p, dbg["qT"], qT[:, :])
            emit_dump_flat(p, dbg["kT"], kT[:, :])
            emit_dump_flat(p, dbg["Mcol"], p.colb[:, 0:16])
            emit_dump_flat(p, dbg["nrm"], nrm[:, :])
        sq3 = sq.rearrange("p (a b) -> p a b", b=S)
        nc.leave_named_scope("b1_c3", sidn, False)
        nc.leave_named_scope("b1_c0123", sid, False)
        sid2 = nc.enter_named_scope("b1_c4", False)[0]
        NGRP = 16
        mm0 = p.s_mm.n
        ev0 = p.s_ev.n
        ve0 = p.s_ve.n
        po0 = p.s_po.n
        sc0 = p.s_sc.n

        def Eap(kt, h):
            g = kt // 2
            return E[:, 4 * (g % 2) + 2 * (kt % 2) + h, :]

        def c4_tensor(tensor):
            itc = 0
            for g in range(NGRP):
                for j in range(2):
                    kt = 2 * g + j
                    for s in range(SC):
                        psA = p.pp[2 * (itc % 2)]
                        psB = p.pp[2 * (itc % 2) + 1]
                        if itc >= 2:
                            tensor.wait_ge(p.s_ev.h, ev0 + 2 * (itc - 1))
                        tensor.matmul(psA[:, :], kT[0:64, bass.ts(kt, 128)],
                                      qT[0:64, bass.ts(s, SW)],
                                      start=True, stop=True).then_inc(p.s_mm.h, 1)
                        tensor.matmul(psB[:, :], kT[64:128, bass.ts(kt, 128)],
                                      qT[64:128, bass.ts(s, SW)],
                                      start=True, stop=True).then_inc(p.s_mm.h, 1)
                        itc += 1
                if g % 2 == 1:
                    u = g // 2
                    for qc in range(SC):
                        ps = p.pq[qc % 2]
                        tensor.wait_ge(p.s_ve.h, ve0 + 5 * (4 * u + 4))
                        if 8 * u + qc >= 2:
                            tensor.wait_ge(p.s_sc.h, sc0 + 8 * u + qc - 1)
                        for jj in range(4):
                            kt = 4 * u + jj
                            tensor.matmul(ps[0:64, :], vpr[:, jj, 0:64],
                                          Eap(kt, 0)[:, bass.ts(qc, SW)],
                                          start=(jj == 0), stop=(jj == 3),
                                          skip_group_check=True)
                            mm = tensor.matmul(ps[64:128, :], vpr[:, jj, 64:128],
                                               Eap(kt, 1)[:, bass.ts(qc, SW)],
                                               start=(jj == 0), stop=(jj == 3),
                                               tile_position=(0, 64),
                                               skip_group_check=True)
                        mm.then_inc(p.s_po.h, 1)

        def c4_scalar(scalar):
            itc = 0
            for g in range(NGRP):
                if g >= 2:
                    scalar.wait_ge(p.s_po.h, po0 + 8 * (g // 2))
                for j in range(2):
                    kt = 2 * g + j
                    for s in range(SC):
                        scalar.wait_ge(p.s_mm.h, mm0 + 2 * itc + 2)
                        psA = p.pp[2 * (itc % 2)]
                        psB = p.pp[2 * (itc % 2) + 1]
                        scalar.activation(Eap(kt, 0)[:, bass.ts(s, SW)], psA[:, :],
                                          AF.Exp, bias=p.colb[:, 8:9]) \
                            .then_inc(p.s_ev.h, 1)
                        scalar.activation(Eap(kt, 1)[:, bass.ts(s, SW)], psB[:, :],
                                          AF.Exp, bias=p.colb[:, 9:10]) \
                            .then_inc(p.s_ev.h, 1)
                        itc += 1

        def c4_vector(vector):
            for u in range(8):
                if u >= 1:
                    vector.wait_ge(p.s_po.h, po0 + 8 * u)
                for jj in range(4):
                    kt = 4 * u + jj
                    vector.wait_ge(p.s_ev.h, ev0 + 16 * (kt + 1))
                    for h in range(2):
                        Ejh = Eap(kt, h)
                        vector.tensor_add(zb1[:, :], Ejh[:, 0:2048],
                                          Ejh[:, 2048:4096])
                        vector.tensor_add(zb2[:, :], zb1[:, 0:1024],
                                          zb1[:, 1024:2048])
                        vector.reduce_sum(nrm[:, 16 + h:17 + h], zb2[:, :],
                                          mybir.AxisListType.X) \
                            .then_inc(p.s_ve.h, 1)
                    vector.wait_ge(p.s_ve.h, ve0 + 5 * kt + 2)
                    vector.reciprocal(nrm[:, 18:20], nrm[:, 16:18]) \
                        .then_inc(p.s_ve.h, 1)
                    vector.wait_ge(p.s_ve.h, ve0 + 5 * kt + 3)
                    vector.tensor_tensor(vpr[:, jj, 0:64], vp[:, kt, 0:64],
                                         nrm[:, 18:19].broadcast_to((128, 64)),
                                         ALU.mult).then_inc(p.s_ve.h, 1)
                    vector.tensor_tensor(vpr[:, jj, 64:128], vp[:, kt, 64:128],
                                         nrm[:, 19:20].broadcast_to((128, 64)),
                                         ALU.mult).then_inc(p.s_ve.h, 1)
                for qc in range(SC):
                    vector.wait_ge(p.s_po.h, po0 + 8 * u + qc + 1)
                    ps = p.pq[qc % 2]
                    if u == 0:
                        vector.tensor_copy(acc[:, bass.ts(qc, SW)], ps[:, :]) \
                            .then_inc(p.s_sc.h, 1)
                    else:
                        vector.tensor_tensor(acc[:, bass.ts(qc, SW)], ps[:, :],
                                             acc[:, bass.ts(qc, SW)], ALU.add) \
                            .then_inc(p.s_sc.h, 1)

        run_block(nc, ("tensor", "scalar", "vector"),
                  {"tensor": c4_tensor, "scalar": c4_scalar,
                   "vector": c4_vector})
        p.s_mm.n = mm0 + 2 * 256
        p.s_ev.n = ev0 + 2 * 256
        p.s_ve.n = ve0 + 5 * 32
        p.s_po.n = po0 + 64
        p.s_sc.n = sc0 + 64
        p.barrier()

        nc.leave_named_scope("b1_c4", sid2, False)
        ve0 = p.s_ve.n
        sg = p.s_gd[0]
        g0 = sg.n

        if dbg and pair == 0:
            emit_dump_flat(p, dbg["acc"], acc[:, :])
            emit_dump_flat(p, dbg["E0"], E[:, 0, :])

        def sp_vec(vector):
            vector.tensor_copy(sq[:, :], acc[:, :]).then_inc(p.s_ve.h, 1)

        def sp_gp(gpsimd, pair=pair):
            gpsimd.wait_ge(p.s_ve.h, ve0 + 1)
            gpsimd.dma_start(outT_spill[pair], sq[:, :]).then_inc(sg.h, 16)
            gpsimd.wait_ge(sg.h, g0 + 16)

        run_block(nc, ("vector", "gpsimd"), {"vector": sp_vec, "gpsimd": sp_gp})
        p.s_ve.n = ve0 + 1
        sg.n = g0 + 16
        p.barrier()

    zT = p.bf3(BIG1, DC, S)
    sg = p.s_gd[0]
    g0 = sg.n

    def rl_gp(gpsimd):
        for pr in range(NPAIR):
            gpsimd.dma_start(zT[:, pr, :], outT_spill[pr]).then_inc(sg.h, 16)
        gpsimd.wait_ge(sg.h, g0 + 16 * NPAIR)

    run_block(nc, ("gpsimd",), {"gpsimd": rl_gp})
    sg.n = g0 + 16 * NPAIR
    p.barrier()

    if dbg:
        emit_dump(p, dbg["zT"], zT)
    wo = p.bf3(RS, DC, D)
    emit_load_bf(p, [(Wb["WoR"][k], wo[:, k, :]) for k in range(DC)])
    z1 = p.bf3(BIG0, DC, S)
    emit_proj(p, n_o=DC, n_k=DC,
              lhsT=lambda o, k: wo[:, k, bass.ts(o, 128)],
              rhs=lambda k, s: zT[:, k, bass.ts(s, SW)],
              evac_cb=lambda eng, o, s, ps: eng.activation(
                  z1[:, o, bass.ts(s, SW)], ps[:, :], AF.Identity))

    if dbg:
        emit_dump(p, dbg["z1"], z1)
    emit_l2norm(p, z1, p.bf3(BIG1, DC, S), sqbig_off=BIG1)

    zn1 = p.bf3(BIG1, DC, S)
    wff = p.bf3(RS, DC, D)
    emit_load_bf(p, [(Wb["WffR"][k], wff[:, k, :]) for k in range(DC)])
    z2 = p.bf3(BIG0, DC, S)
    emit_proj(p, n_o=DC, n_k=DC,
              lhsT=lambda o, k: wff[:, k, bass.ts(o, 128)],
              rhs=lambda k, s: zn1[:, k, bass.ts(s, SW)],
              evac_cb=lambda eng, o, s, ps: eng.activation(
                  z2[:, o, bass.ts(s, SW)], ps[:, :], AF.Identity))

    emit_l2norm(p, z2, p.bf3(BIG1, DC, S), sqbig_off=BIG1, gelu=True)
    out0 = p.bf3(BIG1, DC, S)
    if dbg:
        emit_dump(p, dbg["f1"], out0)

    emit_post_ln_residual(p, out0=out0, osq_off=BIG0, xT_in=xT_in, xT_out=xT_out)


def emit_norm_max(p, sqbuf, nrm, slot):
    nc = p.nc
    mm0, ve0 = p.s_mm.n, p.s_ve.n
    sq3 = sqbuf.rearrange("p (a b) -> p a b", b=S)

    def nt(tensor):
        for s in range(SC):
            psA = p.pp[2 * (s % 2)]
            psB = p.pp[2 * (s % 2) + 1]
            if s >= 2:
                tensor.wait_ge(p.s_ve.h, ve0 + 2 * (s - 1))
            tensor.matmul(psA[:, :], p.ones128[0:64, :],
                          sq3[0:64, 0, bass.ts(s, SW)], start=True, stop=True) \
                .then_inc(p.s_mm.h, 1)
            tensor.matmul(psB[:, :], p.ones128[64:128, :],
                          sq3[64:128, 0, bass.ts(s, SW)], start=True, stop=True) \
                .then_inc(p.s_mm.h, 1)

    def nv(vector):
        for s in range(SC):
            vector.wait_ge(p.s_mm.h, mm0 + 2 * s + 2)
            vector.reduce_max(nrm[:, s:s + 1], p.pp[2 * (s % 2)][:, :],
                              mybir.AxisListType.X)
            vector.reduce_max(nrm[:, 8 + s:9 + s], p.pp[2 * (s % 2) + 1][:, :],
                              mybir.AxisListType.X).then_inc(p.s_ve.h, 2)
        vector.wait_ge(p.s_ve.h, ve0 + 2 * SC)
        vector.reduce_max(nrm[:, 32 + slot:33 + slot], nrm[:, 0:8],
                          mybir.AxisListType.X)
        vector.reduce_max(nrm[:, 33 + slot:34 + slot], nrm[:, 8:16],
                          mybir.AxisListType.X).then_inc(p.s_ve.h, 1)

    run_block(nc, ("tensor", "vector"), {"tensor": nt, "vector": nv})
    p.s_mm.n = mm0 + 2 * SC
    p.s_ve.n = ve0 + 2 * SC + 1
    p.barrier()


def emit_l2norm(p, zin, zout, sqbig_off, gelu=False):
    nc = p.nc
    osq = p.bf3(sqbig_off, DC, S)
    sc0 = p.s_sc.n

    def sq_sc(scalar):
        for d in range(DC):
            scalar.activation(osq[:, d, :], zin[:, d, :], AF.Square)
        scalar.nop().then_inc(p.s_sc.h, 1)

    run_block(nc, ("scalar",), {"scalar": sq_sc})
    p.s_sc.n = sc0 + 1
    p.barrier()

    s2 = p.f32(RS + 8192, 4096)
    r_b = p.f32(RS + 12288, 4096)
    emit_stat_pass(p, osq, s2, inv_n=1.0)
    ve0, sc0 = p.s_ve.n, p.s_sc.n

    def rn_sc(scalar):
        scalar.activation(r_b[:, :], s2[:, :], AF.Ln, bias=1e-30)
        scalar.activation(r_b[:, :], r_b[:, :], AF.Exp, scale=0.5) \
            .then_inc(p.s_sc.h, 1)

    def rn_vec(vector):
        vector.wait_ge(p.s_sc.h, sc0 + 1)
        vector.tensor_scalar_max(r_b[:, :], r_b[:, :], 1e-12) \
            .then_inc(p.s_ve.h, 1)
        vector.wait_ge(p.s_ve.h, ve0 + 1)
        vector.reciprocal(r_b[:, :], r_b[:, :]).then_inc(p.s_ve.h, 1)

    run_block(nc, ("scalar", "vector"), {"scalar": rn_sc, "vector": rn_vec})
    p.s_sc.n = sc0 + 1
    p.s_ve.n = ve0 + 2
    p.barrier()

    t_g = p.f32(RS + 8192, 4096)
    ve0, sc0 = p.s_ve.n, p.s_sc.n

    def ap_vec(vector):
        for d in range(DC):
            if gelu:
                if d >= 1:
                    vector.wait_ge(p.s_sc.h, sc0 + d)
                vector.tensor_mul(t_g[:, :], zin[:, d, :], r_b[:, :]) \
                    .then_inc(p.s_ve.h, 1)
            else:
                vector.tensor_mul(zout[:, d, :], zin[:, d, :], r_b[:, :]) \
                    .then_inc(p.s_ve.h, 1)

    def ap_sc(scalar):
        for d in range(DC):
            scalar.wait_ge(p.s_ve.h, ve0 + d + 1)
            scalar.activation(zout[:, d, :], t_g[:, :], AF.Gelu) \
                .then_inc(p.s_sc.h, 1)

    bodies = {"vector": ap_vec}
    if gelu:
        bodies["scalar"] = ap_sc
    run_block(nc, ("vector", "scalar"), bodies)
    p.s_ve.n = ve0 + DC
    if gelu:
        p.s_sc.n = sc0 + DC
    p.barrier()


def host_block1(inputs):
    Wh = {}
    Wh["b1_WvR"] = np.ascontiguousarray(
        inputs["L0_ffn_Wv"].reshape(DC, 128, D))
    Wh["b1_WoR"] = np.ascontiguousarray(
        inputs["L0_ffn_Wo"].reshape(DC, 128, D))
    Wh["b1_WffR"] = np.ascontiguousarray(
        inputs["L0_ffn_Wff"].reshape(DC, 128, D))
    for nm, key in (("b1_WqP", "L0_ffn_Wq"), ("b1_WkP", "L0_ffn_Wk")):
        w = inputs[key].reshape(DC, 128, NCORE, 128)
        Wh[nm] = np.ascontiguousarray(w.transpose(2, 1, 0, 3)
                                      .reshape(NCORE, 128, D))
    return Wh


def declare_block1(nc):
    t = {}
    t["WvR"] = nc.dram_tensor("b1_WvR", [DC, 128, D], F32, kind="ExternalInput")
    t["WoR"] = nc.dram_tensor("b1_WoR", [DC, 128, D], F32, kind="ExternalInput")
    t["WffR"] = nc.dram_tensor("b1_WffR", [DC, 128, D], F32, kind="ExternalInput")
    t["WqP"] = nc.dram_tensor("b1_WqP", [8, 128, D], F32, kind="ExternalInput")
    t["WkP"] = nc.dram_tensor("b1_WkP", [8, 128, D], F32, kind="ExternalInput")
    tb = {k: mirror_tensor(nc, t[k]) for k in t}
    W = {}
    W["WvR"] = [t["WvR"][k, :, :] for k in range(DC)]
    W["WoR"] = [t["WoR"][k, :, :] for k in range(DC)]
    W["WffR"] = [t["WffR"][k, :, :] for k in range(DC)]
    W["WqP"] = [t["WqP"][pr, :, :] for pr in range(8)]
    W["WkP"] = [t["WkP"][pr, :, :] for pr in range(8)]
    Wb = {}
    Wb["WvR"] = [tb["WvR"][k, :, :] for k in range(DC)]
    Wb["WoR"] = [tb["WoR"][k, :, :] for k in range(DC)]
    Wb["WffR"] = [tb["WffR"][k, :, :] for k in range(DC)]
    Wb["WqP"] = [tb["WqP"][pr, :, :] for pr in range(8)]
    Wb["WkP"] = [tb["WkP"][pr, :, :] for pr in range(8)]
    mp = []
    for nm in t:
        for k in range(t[nm].shape[0]):
            mp.append((t[nm][k, :, :], tb[nm][k, :, :]))
    return W, Wb, mp


def build_block1_program(debug=False):
    nc = bass.Bass("TRN2", target_bir_lowering=False)
    xT_in_t = nc.dram_tensor("xT_in", [DC, 128, S], F32, kind="ExternalInput")
    xT_out_t = nc.dram_tensor("xT_out", [DC, 128, S], F32, kind="ExternalOutput")
    v_sp = nc.dram_tensor("v_sp", [8, 128, 32, 128], BF16, kind="Internal")
    o_sp = nc.dram_tensor("o_sp", [8, 128, S], BF16, kind="Internal")
    W, Wb, mp = declare_block1(nc)
    dbg = None
    if debug:
        t = nc.dram_tensor("dbg_f1", [DC, 128, S], BF16, kind="ExternalOutput")
        dbg = {"f1": [t[d, :, :] for d in range(DC)]}
        for nm in ("zT", "z1"):
            tt = nc.dram_tensor("dbg_" + nm, [DC, 128, S], BF16,
                                kind="ExternalOutput")
            dbg[nm] = [tt[d, :, :] for d in range(DC)]
        dbg["qT"] = nc.dram_tensor("dbg_qT", [128, S], BF16,
                                   kind="ExternalOutput")[:, :]
        dbg["kT"] = nc.dram_tensor("dbg_kT", [128, S], BF16,
                                   kind="ExternalOutput")[:, :]
        dbg["acc"] = nc.dram_tensor("dbg_acc", [128, S], F32,
                                    kind="ExternalOutput")[:, :]
        dbg["E0"] = nc.dram_tensor("dbg_E0", [128, S], BF16,
                                   kind="ExternalOutput")[:, :]
        dbg["Mcol"] = nc.dram_tensor("dbg_Mcol", [128, 16], F32,
                                     kind="ExternalOutput")[:, :]
        dbg["nrm"] = nc.dram_tensor("dbg_nrm", [128, 64], F32,
                                    kind="ExternalOutput")[:, :]
        dbg["nrm2"] = nc.dram_tensor("dbg_nrm2", [128, 64], F32,
                                     kind="ExternalOutput")[:, :]
        dbg["vpr"] = nc.dram_tensor("dbg_vpr", [128, 512], BF16,
                                    kind="ExternalOutput")[:, :]
        dbg["E00"] = nc.dram_tensor("dbg_E00", [128, S], BF16,
                                    kind="ExternalOutput")[:, :]
        dbg["acc0"] = nc.dram_tensor("dbg_acc0", [128, S], F32,
                                     kind="ExternalOutput")[:, :]
        dbg["vp"] = nc.dram_tensor("dbg_vp", [128, S], BF16,
                                   kind="ExternalOutput")[:, :]
    with ExitStack() as ctx:
        p = Prog(nc, ctx)
        emit_mirror_prelude(p, mp, RS)
        emit_block1(p, [xT_in_t[d, :, :] for d in range(DC)],
                    [xT_out_t[d, :, :] for d in range(DC)], W, Wb,
                    [v_sp[pr, :, :, :] for pr in range(8)],
                    [o_sp[pr, :, :] for pr in range(8)], dbg=dbg)
    return nc


def emit_block2(p, xT_in, xT_out, W, Wb, v_spill, outT_spill, dbg=None):
    nc = p.nc
    NPAIR = 8
    NM = 11

    sd0 = p.s_ld[0]
    ld0 = sd0.n

    def init_gp(gpsimd):
        gpsimd.memset(p.ones128[:, :], 1.0)
        gpsimd.memset(p.cst[:, 0:1], LNEPS)
        gpsimd.memset(p.cst[:, 1:2], 1.0)
        gpsimd.memset(p.cst[:, 2:3], 1e-30)

    def init_sync(sync):
        sync.dma_start(idt[:, :], W["idt"]).then_inc(sd0.h, 16)
        sync.dma_start(beta_f[:, :], W["beta"]).then_inc(sd0.h, 16)
        sync.wait_ge(sd0.h, ld0 + 32)

    idt = p.arena[:, RS + 17952:RS + 18016].bitcast(BF16)
    beta_f = p.arena[0:1, RS + 18016:RS + 18192]
    run_block(nc, ("gpsimd", "sync"), {"gpsimd": init_gp, "sync": init_sync})
    sd0.n = ld0 + 32
    p.barrier()

    xn = p.bf3(BIG0, DC, S)
    xs0 = p.f32(BIG1, 4096)
    xs1 = p.f32(BIG1 + 4096, 4096)
    sx2_b = p.f32(BIG1 + 8192, 4096)
    t_ap = p.f32(BIG1 + 12288, 4096)
    m_b = p.f32(RS, 4096)
    r_b = p.f32(RS + 4096, 4096)
    emit_stream_stats(p, xT_in, xn, m_b, r_b, sx2_b, xs0, xs1)
    emit_apply_ln(p, xT_in, xn, m_b, r_b, t_ap, xs0, xs1)

    wk = p.bf3(RS, DC, D)
    emit_load_bf(p, [(Wb["WinKR"][k], wk[:, k, :]) for k in range(DC)])
    k_all = p.bf3(BIG1, 32, D)
    mm0, ev0 = p.s_mm.n, p.s_ev.n

    def kb_tensor(tensor):
        it = 0
        for t in range(32):
            for h in range(2):
                ps = p.pp[it % 2]
                if it >= 2:
                    tensor.wait_ge(p.s_ev.h, ev0 + it - 1)
                for k in range(DC):
                    mm = tensor.matmul(ps[:, :], xn[:, k, bass.ts(t, 128)],
                                       wk[:, k, bass.ts(h, 512)],
                                       start=(k == 0), stop=(k == DC - 1))
                mm.then_inc(p.s_mm.h, 1)
                it += 1

    def kb_scalar(scalar):
        it = 0
        for t in range(32):
            for h in range(2):
                scalar.wait_ge(p.s_mm.h, mm0 + it + 1)
                scalar.activation(k_all[:, t, bass.ts(h, 512)], p.pp[it % 2][:, :],
                                  AF.Identity, scale=0.125).then_inc(p.s_ev.h, 1)
                it += 1

    run_block(nc, ("tensor", "scalar"), {"tensor": kb_tensor, "scalar": kb_scalar})
    p.s_mm.n = mm0 + 64
    p.s_ev.n = ev0 + 64
    p.barrier()
    ve0 = p.s_ve.n

    def k_clip(vector):
        for t in range(32):
            vector.tensor_scalar_max(k_all[:, t, :], k_all[:, t, :], -1.0)
            mm = vector.tensor_scalar_min(k_all[:, t, :], k_all[:, t, :], 1.0)
        mm.then_inc(p.s_ve.h, 1)

    run_block(nc, ("vector",), {"vector": k_clip})
    p.s_ve.n = ve0 + 1
    p.barrier()

    wv = p.bf3(RS, DC, D)
    emit_load_bf(p, [(Wb["WinVR"][k], wv[:, k, :]) for k in range(DC)])
    vt0 = p.bf(RS + 4096, 512)
    vt1 = p.bf(RS + 4352, 512)
    mm0, ev0 = p.s_mm.n, p.s_ev.n
    ad0 = [p.s_ad[0].n, p.s_ad[1].n]

    def vb_tensor(tensor):
        it = 0
        for t in range(32):
            for h in range(2):
                ps = p.pp[it % 2]
                if it >= 2:
                    tensor.wait_ge(p.s_ev.h, ev0 + it - 1)
                for k in range(DC):
                    mm = tensor.matmul(ps[:, :], xn[:, k, bass.ts(t, 128)],
                                       wv[:, k, bass.ts(h, 512)],
                                       start=(k == 0), stop=(k == DC - 1))
                mm.then_inc(p.s_mm.h, 1)
                it += 1

    def vb_scalar(scalar):
        it = 0
        for t in range(32):
            for h in range(2):
                scalar.wait_ge(p.s_mm.h, mm0 + it + 1)
                if it >= 2:
                    scalar.wait_ge(p.s_ad[it % 2].h,
                                   ad0[it % 2] + 16 * 4 * (it // 2))
                vt = (vt0, vt1)[it % 2]
                scalar.activation(vt[:, :], p.pp[it % 2][:, :], AF.Identity) \
                    .then_inc(p.s_ev.h, 1)
                scalar.wait_ge(p.s_ev.h, ev0 + it + 1)
                for pj in range(4):
                    pair = h * 4 + pj
                    scalar.dma_start(v_spill[pair][:, t, :],
                                     vt[:, bass.ts(pj, 128)]) \
                        .then_inc(p.s_ad[it % 2].h, 16)
                it += 1
        scalar.wait_ge(p.s_ad[0].h, ad0[0] + 16 * 4 * 32)
        scalar.wait_ge(p.s_ad[1].h, ad0[1] + 16 * 4 * 32)

    run_block(nc, ("tensor", "scalar"),
              {"tensor": vb_tensor, "scalar": vb_scalar})
    p.s_mm.n = mm0 + 64
    p.s_ev.n = ev0 + 64
    p.s_ad[0].n = ad0[0] + 16 * 4 * 32
    p.s_ad[1].n = ad0[1] + 16 * 4 * 32
    p.barrier()

    wqp = p.bf3(RS, DC, 128)
    qp = p.bf(RS + 512, 4096)
    tq = p.bf3(RS + 2560, NM, 1024)
    kf = p.bf(RS + 8192, 5632).rearrange("p (g m c) -> p g m c", m=NM, c=128)
    vp = p.bf3(RS + 11008, 32, 128)
    kvT2 = p.bf3(RS + 13056, NM, 64)
    kve = p.f32(RS + 13440, 704)
    kvb = p.bf(RS + 14144, 704)
    bbc = p.f32(RS + 14496, 704)
    otp = p.bf(RS + 15200, 4096)
    bRowA = p.bf(RS + 17248, 704)
    bRowB = p.bf(RS + 17600, 704)
    tmpq = p.bf(RS + 8192, 2048)

    for pair in range(NPAIR):
        emit_load_bf(p, [(Wb["WinQP"][pair],
                          wqp[:, :, :].rearrange("p a b -> p (a b)"))])
        sg = p.s_gd[0]
        g0 = sg.n

        def vp_gp(gpsimd, pair=pair):
            gpsimd.dma_start(vp[:, :, :], v_spill[pair][:, :, :]).then_inc(sg.h, 16)
            gpsimd.wait_ge(sg.h, g0 + 16)

        run_block(nc, ("gpsimd",), {"gpsimd": vp_gp})
        sg.n = g0 + 16
        p.barrier()

        ve0 = p.s_ve.n

        def beta_vec(vector, pair=pair):
            hA, hB = 2 * pair, 2 * pair + 1
            vector.tensor_copy(
                bRowA[0:1, :].rearrange("o (m c) -> o m c", c=64),
                beta_f[0:1, hA * 11:(hA + 1) * 11].unsqueeze(2)
                .broadcast_to((1, 11, 64)))
            vector.tensor_copy(
                bRowB[0:1, :].rearrange("o (m c) -> o m c", c=64),
                beta_f[0:1, hB * 11:(hB + 1) * 11].unsqueeze(2)
                .broadcast_to((1, 11, 64))).then_inc(p.s_ve.h, 1)

        run_block(nc, ("vector",), {"vector": beta_vec})
        p.s_ve.n = ve0 + 1
        p.barrier()

        mm0, ev0 = p.s_mm.n, p.s_ev.n

        def bb_tensor(tensor):
            for seg, (s0, s1) in enumerate(((0, 512), (512, 704))):
                tensor.matmul(p.pq[seg][0:64, 0:s1 - s0], p.ones128[0:1, 0:64],
                              bRowA[0:1, s0:s1], start=True, stop=True,
                              skip_group_check=True)
                mm = tensor.matmul(p.pq[seg][64:128, 0:s1 - s0],
                                   p.ones128[0:1, 0:64],
                                   bRowB[0:1, s0:s1], start=True, stop=True,
                                   tile_position=(0, 64), skip_group_check=True)
            mm.then_inc(p.s_mm.h, 1)

        def bb_scalar(scalar):
            scalar.wait_ge(p.s_mm.h, mm0 + 1)
            scalar.activation(bbc[:, 0:512], p.pq[0][:, :], AF.Identity)
            scalar.activation(bbc[:, 512:704], p.pq[1][:, 0:192], AF.Identity) \
                .then_inc(p.s_ev.h, 1)

        run_block(nc, ("tensor", "scalar"), {"tensor": bb_tensor, "scalar": bb_scalar})
        p.s_mm.n = mm0 + 1
        p.s_ev.n = ev0 + 1
        p.barrier()

        emit_proj(p, n_o=1, n_k=DC,
                  lhsT=lambda o, k: wqp[:, k, :],
                  rhs=lambda k, s: xn[:, k, bass.ts(s, SW)],
                  evac_cb=lambda eng, o, s, ps: eng.activation(
                      qp[:, bass.ts(s, SW)], ps[:, :], AF.Identity, scale=0.125))
        ve0 = p.s_ve.n

        def q_clip(vector):
            vector.tensor_scalar_max(qp[:, :], qp[:, :], -1.0)
            vector.tensor_scalar_min(qp[:, :], qp[:, :], 1.0) \
                .then_inc(p.s_ve.h, 1)

        run_block(nc, ("vector",), {"vector": q_clip})
        p.s_ve.n = ve0 + 1
        p.barrier()

        mm0, ve0 = p.s_mm.n, p.s_ve.n

        def kv_vector(vector, pair=pair, ve0=ve0, mm0=mm0):
            for sg_i in range(8):
                if sg_i >= 1:
                    vector.wait_ge(p.s_mm.h, mm0 + 4 * sg_i)
                ksl = k_all[:, 4 * sg_i:4 * sg_i + 4,
                            bass.ts(pair, 128)]
                vector.memset(kf[:, :, 0, :], 1.0)
                vector.tensor_copy(kf[:, :, 1, :], ksl)
                for m in range(2, NM):
                    vector.tensor_mul(kf[:, :, m, :], ksl, kf[:, :, m - 1, :])
                    mm = vector.scalar_tensor_tensor(kf[:, :, m, :],
                                                     kf[:, :, m, :], 2.0,
                                                     kf[:, :, m - 2, :],
                                                     ALU.mult, ALU.subtract)
                mm.then_inc(p.s_ve.h, 1)

        def kv_tensor(tensor, ve0=ve0):
            for sg_i in range(8):
                tensor.wait_ge(p.s_ve.h, ve0 + sg_i + 1)
                for ti in range(4):
                    t = 4 * sg_i + ti
                    first = (t == 0)
                    last = (t == 31)
                    for seg, (m0, m1) in enumerate(((0, 8), (8, 11))):
                        w_ = (m1 - m0) * 64
                        tensor.matmul(p.pq[seg][0:64, 0:w_], vp[:, t, 0:64],
                                      kf[:, ti, m0:m1, 0:64],
                                      start=first, stop=last,
                                      skip_group_check=True)
                        mm = tensor.matmul(p.pq[seg][64:128, 0:w_],
                                           vp[:, t, 64:128],
                                           kf[:, ti, m0:m1, 64:128],
                                           start=first, stop=last,
                                           tile_position=(0, 64),
                                           skip_group_check=True)
                    mm.then_inc(p.s_mm.h, 1)

        run_block(nc, ("vector", "tensor"),
                  {"vector": kv_vector, "tensor": kv_tensor})
        p.s_ve.n = ve0 + 8
        p.s_mm.n = mm0 + 32
        p.barrier()

        sc0, ve0 = p.s_sc.n, p.s_ve.n

        def kv_evac_sc(scalar):
            scalar.activation(kve[:, 0:512], p.pq[0][:, :], AF.Identity)
            scalar.activation(kve[:, 512:704], p.pq[1][:, 0:192], AF.Identity) \
                .then_inc(p.s_sc.h, 1)

        def kv_beta_vec(vector):
            vector.wait_ge(p.s_sc.h, sc0 + 1)
            vector.tensor_mul(kvb[:, :], kve[:, :], bbc[:, :]) \
                .then_inc(p.s_ve.h, 1)

        run_block(nc, ("scalar", "vector"),
                  {"scalar": kv_evac_sc, "vector": kv_beta_vec})
        p.s_sc.n = sc0 + 1
        p.s_ve.n = ve0 + 1
        p.barrier()

        kvb3 = kvb.rearrange("p (m c) -> p m c", c=64)
        mm0, ev0 = p.s_mm.n, p.s_ev.n

        def tr_tensor(tensor):
            n = 0
            for h in range(2):
                for m in range(NM):
                    if n >= 1:
                        tensor.wait_ge(p.s_ev.h, ev0 + n)
                    tensor.transpose(p.ps_tr[0:64, 0:64],
                                     kvb3[64 * h:64 * h + 64, m, :],
                                     idt[64 * h:64 * h + 64, 64 * h:64 * h + 64]) \
                        .then_inc(p.s_mm.h, 1)
                    n += 1


        def tr_scalar(scalar):
            n = 0
            for h in range(2):
                for m in range(NM):
                    scalar.wait_ge(p.s_mm.h, mm0 + n + 1)
                    scalar.activation(kvT2[64 * h:64 * h + 64, m, :],
                                      p.ps_tr[0:64, 0:64], AF.Identity) \
                        .then_inc(p.s_ev.h, 1)
                    n += 1

        run_block(nc, ("tensor", "scalar"),
                  {"tensor": tr_tensor, "scalar": tr_scalar})
        p.s_mm.n = mm0 + 2 * NM
        p.s_ev.n = ev0 + 2 * NM
        p.barrier()

        for qtr in range(4):
            ve0, mm0, ev0 = p.s_ve.n, p.s_mm.n, p.s_ev.n
            Q = qp.rearrange("p (q w) -> p q w", w=1024)[:, qtr, :]

            def tq_vec(vector, qtr=qtr, Q=Q):
                vector.memset(tq[:, 0, :], 1.0)
                vector.tensor_copy(tq[:, 1, :], Q)
                for m in range(2, NM):
                    vector.tensor_mul(tmpq[:, 0:1024], Q, tq[:, m - 1, :])
                    mm = vector.scalar_tensor_tensor(tq[:, m, :], tmpq[:, 0:1024],
                                                     2.0, tq[:, m - 2, :],
                                                     ALU.mult, ALU.subtract)
                mm.then_inc(p.s_ve.h, 1)

            def o_tensor(tensor, qtr=qtr, ve0=ve0, ev0=ev0):
                tensor.wait_ge(p.s_ve.h, ve0 + 1)
                for c in range(2):
                    ps = p.pp[c % 2]
                    if qtr * 2 + c >= 2:
                        tensor.wait_ge(p.s_ev.h, ev0 + c - 1)
                    for m in range(NM):
                        tensor.matmul(ps[0:64, :], kvT2[0:64, m, :],
                                      tq[0:64, m, bass.ts(c, SW)],
                                      start=(m == 0), stop=(m == NM - 1),
                                      skip_group_check=True)
                        mm = tensor.matmul(ps[64:128, :], kvT2[64:128, m, :],
                                           tq[64:128, m, bass.ts(c, SW)],
                                           start=(m == 0), stop=(m == NM - 1),
                                           tile_position=(64, 64),
                                           skip_group_check=True)
                    mm.then_inc(p.s_mm.h, 1)

            def o_scalar(scalar, qtr=qtr, mm0=mm0):
                for c in range(2):
                    scalar.wait_ge(p.s_mm.h, mm0 + c + 1)
                    scalar.activation(
                        otp[:, qtr * 1024 + c * SW:qtr * 1024 + (c + 1) * SW],
                        p.pp[c % 2][:, :], AF.Identity).then_inc(p.s_ev.h, 1)

            run_block(nc, ("vector", "tensor", "scalar"),
                      {"vector": tq_vec, "tensor": o_tensor, "scalar": o_scalar})
            p.s_ve.n = ve0 + 1
            p.s_mm.n = mm0 + 2
            p.s_ev.n = ev0 + 2
            p.barrier()

        sc0 = p.s_sc.n
        sa = p.s_ad[0]
        a0 = sa.n

        def sp_sc(scalar, pair=pair):
            scalar.dma_start(outT_spill[pair], otp[:, :]).then_inc(sa.h, 16)
            scalar.wait_ge(sa.h, a0 + 16)

        run_block(nc, ("scalar",), {"scalar": sp_sc})
        sa.n = a0 + 16
        p.barrier()

    zT = p.bf3(BIG1, DC, S)
    sg = p.s_gd[0]
    g0 = sg.n

    def rl_gp(gpsimd):
        for pr in range(NPAIR):
            gpsimd.dma_start(zT[:, pr, :], outT_spill[pr]).then_inc(sg.h, 16)
        gpsimd.wait_ge(sg.h, g0 + 16 * NPAIR)

    run_block(nc, ("gpsimd",), {"gpsimd": rl_gp})
    sg.n = g0 + 16 * NPAIR
    p.barrier()

    wo = p.bf3(RS, DC, D)
    emit_load_bf(p, [(Wb["WoutR"][k], wo[:, k, :]) for k in range(DC)])
    out0 = p.bf3(BIG0, DC, S)
    emit_proj(p, n_o=DC, n_k=DC,
              lhsT=lambda o, k: wo[:, k, bass.ts(o, 128)],
              rhs=lambda k, s: zT[:, k, bass.ts(s, SW)],
              evac_cb=lambda eng, o, s, ps: eng.activation(
                  out0[:, o, bass.ts(s, SW)], ps[:, :], AF.Identity))
    if dbg:
        emit_dump(p, dbg["f2"], out0)

    emit_post_ln_residual(p, out0=out0, osq_off=BIG1, xT_in=xT_in, xT_out=xT_out)


def host_block2(inputs):
    Wh = {}
    w_in = inputs["L1_attn_w_in"]
    Wh["b2_WinKR"] = np.ascontiguousarray(
        w_in[1024:2048, :].T.reshape(DC, 128, D))
    Wh["b2_WinVR"] = np.ascontiguousarray(
        w_in[2048:3072, :].T.reshape(DC, 128, D))
    wq = w_in[0:1024, :]
    t4 = wq.reshape(NCORE, 128, DC, 128)
    wqT = wq.T.reshape(DC, 128, NCORE, 128)
    Wh["b2_WinQP"] = np.ascontiguousarray(
        wqT.transpose(2, 1, 0, 3).reshape(NCORE, 128, D))
    Wh["b2_WoutR"] = rowsT(inputs["L1_attn_w_out"])
    Wh["b2_beta"] = np.ascontiguousarray(
        inputs["L1_attn_beta"].reshape(1, 176))
    Wh["b2_idt"] = np.eye(128, dtype=BF)
    return Wh


def declare_block2(nc):
    t = {}
    t["WinKR"] = nc.dram_tensor("b2_WinKR", [DC, 128, D], F32, kind="ExternalInput")
    t["WinVR"] = nc.dram_tensor("b2_WinVR", [DC, 128, D], F32, kind="ExternalInput")
    t["WinQP"] = nc.dram_tensor("b2_WinQP", [8, 128, D], F32, kind="ExternalInput")
    t["WoutR"] = nc.dram_tensor("b2_WoutR", [DC, 128, D], F32, kind="ExternalInput")
    t["beta"] = nc.dram_tensor("b2_beta", [1, 176], F32, kind="ExternalInput")
    t["idt"] = nc.dram_tensor("b2_idt", [128, 128], BF16, kind="ExternalInput")
    tb = {k: mirror_tensor(nc, t[k]) for k in
          ("WinKR", "WinVR", "WinQP", "WoutR")}
    W = {}
    W["WinKR"] = [t["WinKR"][k, :, :] for k in range(DC)]
    W["WinVR"] = [t["WinVR"][k, :, :] for k in range(DC)]
    W["WinQP"] = [t["WinQP"][pr, :, :] for pr in range(8)]
    W["WoutR"] = [t["WoutR"][k, :, :] for k in range(DC)]
    W["beta"] = t["beta"][:, :]
    W["idt"] = t["idt"][:, :]
    Wb = {}
    Wb["WinKR"] = [tb["WinKR"][k, :, :] for k in range(DC)]
    Wb["WinVR"] = [tb["WinVR"][k, :, :] for k in range(DC)]
    Wb["WinQP"] = [tb["WinQP"][pr, :, :] for pr in range(8)]
    Wb["WoutR"] = [tb["WoutR"][k, :, :] for k in range(DC)]
    mp = []
    for nm in ("WinKR", "WinVR", "WinQP", "WoutR"):
        for k in range(t[nm].shape[0]):
            mp.append((t[nm][k, :, :], tb[nm][k, :, :]))
    return W, Wb, mp


def build_block2_program(debug=False):
    nc = bass.Bass("TRN2", target_bir_lowering=False)
    xT_in_t = nc.dram_tensor("xT_in", [DC, 128, S], F32, kind="ExternalInput")
    xT_out_t = nc.dram_tensor("xT_out", [DC, 128, S], F32, kind="ExternalOutput")
    v_sp = nc.dram_tensor("v_sp", [8, 128, 32, 128], BF16, kind="Internal")
    o_sp = nc.dram_tensor("o_sp", [8, 128, S], BF16, kind="Internal")
    W, Wb, mp = declare_block2(nc)
    dbg = None
    if debug:
        t = nc.dram_tensor("dbg_f2", [DC, 128, S], BF16, kind="ExternalOutput")
        dbg = {"f2": [t[d, :, :] for d in range(DC)]}
    with ExitStack() as ctx:
        p = Prog(nc, ctx)
        emit_mirror_prelude(p, mp, RS)
        emit_block2(p, [xT_in_t[d, :, :] for d in range(DC)],
                    [xT_out_t[d, :, :] for d in range(DC)], W, Wb,
                    [v_sp[pr, :, :, :] for pr in range(8)],
                    [o_sp[pr, :, :] for pr in range(8)], dbg=dbg)
    return nc

def rowsT(w):
    K = w.shape[1]
    return np.ascontiguousarray(w.T.reshape(K // 128, 128, w.shape[0]))


def colsv(b):
    return np.ascontiguousarray(b.reshape(-1, 128, 1))


def host_block0(inputs):
    Wh = {}
    Wh["b0_w1T"] = rowsT(inputs["L0_attn_w1"])
    Wh["b0_w2T"] = rowsT(inputs["L0_attn_w2"])
    Wh["b0_w3T"] = rowsT(inputs["L0_attn_w3"])
    Wh["b0_dlT"] = rowsT(inputs["L0_attn_dl_w"])
    Wh["b0_AT"] = rowsT(inputs["L0_attn_A_w"])
    Wh["b0_BT"] = rowsT(inputs["L0_attn_B_w"])
    Wh["b0_CT"] = rowsT(inputs["L0_attn_C_w"])
    Wh["b0_DT"] = rowsT(inputs["L0_attn_D_w"])
    Wh["b0_convw"] = np.ascontiguousarray(
        inputs["L0_attn_conv_w"].reshape(8, 128, 3).transpose(1, 0, 2)
        .reshape(128, 24))
    Wh["b0_convb"] = np.ascontiguousarray(
        inputs["L0_attn_conv_b"].reshape(8, 128).T)
    Wh["b0_dlb"] = colsv(inputs["L0_attn_dl_b"])
    Wh["b0_Ab"] = colsv(inputs["L0_attn_A_b"])
    Wh["b0_Bb"] = colsv(inputs["L0_attn_B_b"])
    Wh["b0_CDb"] = np.ascontiguousarray(np.concatenate([
        inputs["L0_attn_C_b"].reshape(8, 128).T,
        inputs["L0_attn_D_b"].reshape(8, 128).T], axis=1))
    return Wh


def declare_block0(nc):
    t = {}
    t["w1T"] = nc.dram_tensor("b0_w1T", [DC, 128, D], F32, kind="ExternalInput")
    t["w2T"] = nc.dram_tensor("b0_w2T", [DC, 128, D], F32, kind="ExternalInput")
    t["w3T"] = nc.dram_tensor("b0_w3T", [DC, 128, D], F32, kind="ExternalInput")
    t["dlT"] = nc.dram_tensor("b0_dlT", [DC, 128, HID], F32, kind="ExternalInput")
    t["AT"] = nc.dram_tensor("b0_AT", [DC, 128, HID], F32, kind="ExternalInput")
    t["BT"] = nc.dram_tensor("b0_BT", [DC, 128, HID], F32, kind="ExternalInput")
    t["CT"] = nc.dram_tensor("b0_CT", [HC, 128, D], F32, kind="ExternalInput")
    t["DT"] = nc.dram_tensor("b0_DT", [DC, 128, D], F32, kind="ExternalInput")
    t["convw"] = nc.dram_tensor("b0_convw", [128, 24], F32, kind="ExternalInput")
    t["convb"] = nc.dram_tensor("b0_convb", [128, 8], F32, kind="ExternalInput")
    t["dlb"] = nc.dram_tensor("b0_dlb", [HC, 128, 1], F32, kind="ExternalInput")
    t["Ab"] = nc.dram_tensor("b0_Ab", [HC, 128, 1], F32, kind="ExternalInput")
    t["Bb"] = nc.dram_tensor("b0_Bb", [HC, 128, 1], F32, kind="ExternalInput")
    t["CDb"] = nc.dram_tensor("b0_CDb", [128, 16], F32, kind="ExternalInput")
    tb = {k: mirror_tensor(nc, t[k]) for k in
          ("w1T", "w2T", "w3T", "dlT", "AT", "BT", "CT", "DT")}

    W = {}
    W["w1T"] = [t["w1T"][k, :, :] for k in range(DC)]
    W["w2T"] = [t["w2T"][k, :, :] for k in range(DC)]
    W["w3T"] = [t["w3T"][k, :, :] for k in range(DC)]
    W["dlT"] = [[t["dlT"][k, :, bass.ts(c, 128)] for c in range(HC)]
                for k in range(DC)]
    W["AT"] = [[t["AT"][k, :, bass.ts(c, 128)] for c in range(HC)]
               for k in range(DC)]
    W["BT"] = [[t["BT"][k, :, bass.ts(c, 128)] for c in range(HC)]
               for k in range(DC)]
    W["CT"] = [t["CT"][k, :, :] for k in range(HC)]
    W["DT"] = [t["DT"][k, :, :] for k in range(DC)]
    W["conv_w"] = t["convw"][:, :]
    W["conv_b8"] = t["convb"][:, :]
    W["dlb"] = [t["dlb"][c, :, :] for c in range(HC)]
    W["Ab"] = [t["Ab"][c, :, :] for c in range(HC)]
    W["Bb"] = [t["Bb"][c, :, :] for c in range(HC)]
    W["CDb"] = t["CDb"][:, :]
    Wb = {}
    Wb["w1T"] = [tb["w1T"][k, :, :] for k in range(DC)]
    Wb["w2T"] = [tb["w2T"][k, :, :] for k in range(DC)]
    Wb["w3T"] = [tb["w3T"][k, :, :] for k in range(DC)]
    Wb["dlT"] = [[tb["dlT"][k, :, bass.ts(c, 128)] for c in range(HC)]
                 for k in range(DC)]
    Wb["AT"] = [[tb["AT"][k, :, bass.ts(c, 128)] for c in range(HC)]
                for k in range(DC)]
    Wb["BT"] = [[tb["BT"][k, :, bass.ts(c, 128)] for c in range(HC)]
                for k in range(DC)]
    Wb["CT"] = [tb["CT"][k, :, :] for k in range(HC)]
    Wb["DT"] = [tb["DT"][k, :, :] for k in range(DC)]
    mp = []
    for nm in ("w1T", "w2T", "w3T", "dlT", "AT", "BT", "CT", "DT"):
        for k in range(t[nm].shape[0]):
            mp.append((t[nm][k, :, :], tb[nm][k, :, :]))
    return W, Wb, mp


def build_block0_program(debug=False):
    nc = bass.Bass("TRN2", target_bir_lowering=False)
    xT_in_t = nc.dram_tensor("xT_in", [DC, 128, S], F32, kind="ExternalInput")
    xT_out_t = nc.dram_tensor("xT_out", [DC, 128, S], F32, kind="ExternalOutput")
    hkind = "ExternalOutput" if debug else "Internal"
    h_spill_t = nc.dram_tensor("h_spill", [HC, 128, S], BF16, kind=hkind)
    W, Wb, mp = declare_block0(nc)
    dbg = None
    if debug:
        dbg = {}
        for name in ("xn", "x0", "x1", "ssm", "out0"):
            t = nc.dram_tensor("dbg_" + name, [DC, 128, S], BF16,
                               kind="ExternalOutput")
            dbg[name] = [t[d, :, :] for d in range(DC)]
    with ExitStack() as ctx:
        p = Prog(nc, ctx)
        emit_mirror_prelude(p, mp, RS)
        emit_block0(p, [xT_in_t[d, :, :] for d in range(DC)],
                    [xT_out_t[d, :, :] for d in range(DC)], W, Wb,
                    [h_spill_t[c, :, :] for c in range(HC)], dbg=dbg)
    return nc


def build_fused_program():
    nc = bass.Bass("TRN2", target_bir_lowering=False)
    xT_in_t = nc.dram_tensor("xT_in", [DC, 128, S], F32, kind="ExternalInput")
    xT_out_t = nc.dram_tensor("xT_out", [DC, 128, S], F32, kind="ExternalOutput")
    x1T = nc.dram_tensor("x1T", [DC, 128, S], F32, kind="Internal")
    x2T = nc.dram_tensor("x2T", [DC, 128, S], F32, kind="Internal")
    x3T = nc.dram_tensor("x3T", [DC, 128, S], F32, kind="Internal")
    h_spill_t = nc.dram_tensor("h_spill", [HC, 128, S], BF16, kind="Internal")
    v_sp = nc.dram_tensor("v_sp", [8, 128, 32, 128], BF16, kind="Internal")
    o_sp = nc.dram_tensor("o_sp", [8, 128, S], BF16, kind="Internal")
    W0, Wb0, mp0 = declare_block0(nc)
    W1, Wb1, mp1 = declare_block1(nc)
    W2, Wb2, mp2 = declare_block2(nc)
    W3, Wb3, mp3 = declare_block3(nc)
    xin = [xT_in_t[d, :, :] for d in range(DC)]
    x1 = [x1T[d, :, :] for d in range(DC)]
    x2 = [x2T[d, :, :] for d in range(DC)]
    x3 = [x3T[d, :, :] for d in range(DC)]
    xout = [xT_out_t[d, :, :] for d in range(DC)]
    vs = [v_sp[pr, :, :, :] for pr in range(8)]
    os_ = [o_sp[pr, :, :] for pr in range(8)]
    with ExitStack() as ctx:
        p = Prog(nc, ctx)
        emit_mirror_prelude(p, mp0 + mp1 + mp2 + mp3, RS)
        with nc.named_scope("blk0"):
            emit_block0(p, xin, x1, W0, Wb0,
                        [h_spill_t[c, :, :] for c in range(HC)])
        with nc.named_scope("blk1"):
            emit_block1(p, x1, x2, W1, Wb1, vs, os_)
        with nc.named_scope("blk2"):
            emit_block2(p, x2, x3, W2, Wb2, vs, os_)
        with nc.named_scope("blk3"):
            emit_block3(p, x3, xout, W3, Wb3)
    return nc


def kernel(**inputs):
    x = np.asarray(inputs["L0_attn_x"], dtype=np.float32)
    Wh = {}
    Wh.update(host_block0(inputs))
    Wh.update(host_block1(inputs))
    Wh.update(host_block2(inputs))
    Wh.update(host_block3(inputs))
    nc = build_fused_program()
    in_maps = []
    for b in range(NCORE):
        m = dict(Wh)
        m["xT_in"] = np.ascontiguousarray(x[b].T).reshape(DC, 128, S)
        in_maps.append(m)
    res = run_bass_kernel_spmd(nc, in_maps, core_ids=list(range(NCORE)))
    out = np.empty((NCORE, S, D), dtype=np.float32)
    for b in range(NCORE):
        out[b] = res.results[b]["xT_out"].reshape(D, S).T
    return out
